# version 24
# speedup vs baseline: 1.0153x; 1.0078x over previous
import math
import numpy as np
import ml_dtypes

import concourse.bass as bass
import concourse.mybir as mybir
from concourse.bass_utils import run_bass_kernel_spmd

F32 = mybir.dt.float32
BF16 = mybir.dt.bfloat16
F16 = mybir.dt.float16
I32 = mybir.dt.int32
AF = mybir.ActivationFunctionType
ALU = mybir.AluOpType
AX = mybir.AxisListType
bfnp = ml_dtypes.bfloat16

N_CORES = 8
D = 2048
B_SZ = 4
S = 2048
T = 1024
NDT = D // 128


class Builder:
    ENGS = ("tensor", "vector", "scalar", "gpsimd", "sync")

    def __init__(self, nc):
        self.nc = nc
        self.items = []
        self.n_ops = {e: 0 for e in self.ENGS}
        self.writer = {}
        self.readers = {}
        self.need_inc = {e: set() for e in self.ENGS}
        self.chans = {}

    def op(self, eng, fn, reads=(), writes=(), dma_chan=None, cc_chan=None,
           tag=""):
        ordn = self.n_ops[eng]
        self.n_ops[eng] += 1
        if cc_chan is not None:
            dma_chan = cc_chan
            self.chan_amt = getattr(self, "chan_amt", {})
            self.chan_amt[cc_chan] = 1
        if dma_chan is not None:
            seq = self.chans.get(dma_chan, 0) + 1
            self.chans[dma_chan] = seq
            tok = ("D", dma_chan, seq, eng, ordn)
        else:
            tok = ("E", eng, ordn)
            self.last_compute = getattr(self, "last_compute", {})
            self.last_compute[eng] = ordn
        deps = []
        if eng == "scalar" and dma_chan is None and "cpool" not in reads:
            reads = list(reads) + ["cpool"]
        fw = self.writer.get("__fence__")
        if fw is not None:
            deps.append(fw)
        for r in reads:
            w = self.writer.get(r)
            if w is not None:
                deps.append(w)
            self.readers.setdefault(r, []).append(tok)
        for r in writes:
            w = self.writer.get(r)
            if w is not None:
                deps.append(w)
            for rd in self.readers.get(r, ()):
                deps.append(rd)
            self.readers[r] = []
            self.writer[r] = tok
        best = {}
        for d in deps:
            if d == tok:
                continue
            if eng == "tensor" and d[0] == "E" and d[1] == "tensor":
                continue
            if d[0] == "E":
                key = ("E", d[1])
                if key not in best or best[key][2] < d[2]:
                    best[key] = d
            else:
                key = ("D", d[1])
                if key not in best or best[key][2] < d[2]:
                    best[key] = d
        dep_list = sorted(best.values(), key=lambda d: (d[0], d[1], d[2]))
        for d in dep_list:
            if d[0] == "E":
                self.need_inc[d[1]].add(d[2])
        self.items.append(dict(kind="op", eng=eng, ordn=ordn, fn=fn,
                               deps=dep_list, tok=tok, tag=tag))
        return tok

    def fence(self):
        deps = []
        lc = getattr(self, "last_compute", {})
        for e in self.ENGS:
            if e in lc:
                deps.append(("E", e, lc[e]))
                self.need_inc[e].add(lc[e])
        for ch, cnt in self.chans.items():
            if ch.startswith("cc_"):
                continue
            deps.append(("D", ch, cnt, None, None))
        ordn = self.n_ops["gpsimd"]
        self.n_ops["gpsimd"] += 1
        tok = ("E", "gpsimd", ordn)
        best = {}
        for d in deps:
            if d[0] == "E" and d[1] == "gpsimd" and d[2] == ordn:
                continue
            key = (d[0], d[1])
            if key not in best or best[key][2] < d[2]:
                best[key] = d
        dep_list = sorted(best.values(), key=lambda d: (d[0], d[1], d[2]))
        self.items.append(dict(kind="op", eng="gpsimd", ordn=ordn,
                               fn=lambda e: e.engine_nop(), deps=dep_list,
                               tok=tok, tag="fence"))
        self.need_inc["gpsimd"].add(ordn)
        self._fence_tok = tok
        self.writer["__fence__"] = tok
        return tok

    def finalize(self, sems, final_waits=()):
        semval = {e: {} for e in self.ENGS}
        total = {e: 0 for e in self.ENGS}
        amts = getattr(self, "chan_amt", {})
        for it in self.items:
            tok = it["tok"]
            if tok[0] == "D":
                it["inc"] = (tok[1], amts.get(tok[1], 16))
            else:
                e, o = tok[1], tok[2]
                if o in self.need_inc[e]:
                    total[e] += 1
                    it["inc"] = (e, 1)
                    semval[e][o] = total[e]
                else:
                    it["inc"] = None
        inc_ords = {e: sorted(semval[e]) for e in self.ENGS}

        def wait_spec(d):
            if d[0] == "D":
                return (d[1], amts.get(d[1], 16) * d[2])
            import bisect
            e, o = d[1], d[2]
            lst = inc_ords[e]
            i = bisect.bisect_left(lst, o)
            assert i < len(lst), f"no inc covering {e}@{o}"
            return (e, semval[e][lst[i]])

        per_eng = {e: [] for e in self.ENGS}
        watermark = {}
        for it in self.items:
            e = it["eng"]
            for d in it["deps"]:
                sname, v = wait_spec(d)
                key = (e, sname)
                if watermark.get(key, -1) >= v:
                    continue
                watermark[key] = v
                per_eng[e].append(("wait", sname, v))
            per_eng[e].append(("op", it))
        for (e, d) in final_waits:
            if d[0] == "D":
                sname, v = d[1], amts.get(d[1], 16) * self.chans[d[1]]
            else:
                sname, v = wait_spec(d)
            if watermark.get((e, sname), -1) < v:
                watermark[(e, sname)] = v
                per_eng[e].append(("wait", sname, v))
        self._per_eng = per_eng

        def run(eng_name, eng_obj):
            for item in per_eng[eng_name]:
                if item[0] == "wait":
                    _, sname, v = item
                    eng_obj.wait_ge(sems[sname], v)
                else:
                    it = item[1]
                    ins = it["fn"](eng_obj)
                    if it["inc"] is not None:
                        sname, amt = it["inc"]
                        ins.then_inc(sems[sname], amt)
        return run


def make_program(nc, emit_fn):
    from contextlib import ExitStack
    st = ExitStack()
    b = Builder(nc)
    b._stack = st
    final_waits = emit_fn(b)
    import os as _os
    if _os.environ.get("MK_DBG_CHANS"):
        print("MK_DBG n_chans =", len(b.chans), sorted(b.chans))
    sems = {e: st.enter_context(nc.semaphore(f"sem_{e}")) for e in Builder.ENGS}
    for ch in b.chans:
        sems[ch] = st.enter_context(nc.semaphore(f"sem_{ch}"))
    block = st.enter_context(nc.Block())
    run = b.finalize(sems, final_waits)
    with st:
        @block.tensor
        def _(eng):
            run("tensor", eng)

        @block.vector
        def _(eng):
            run("vector", eng)

        @block.scalar
        def _(eng):
            run("scalar", eng)

        @block.gpsimd
        def _(eng):
            run("gpsimd", eng)

        @block.sync
        def _(eng):
            run("sync", eng)
    return nc


SCALE1 = 64.0
I0 = 5632
I1 = 8192


def _hadamard(n):
    h = np.array([[1.0]], dtype=np.float32)
    while h.shape[0] < n:
        h = np.block([[h, h], [h, -h]])
    return h


def _col_tiles(vec):
    v = np.asarray(vec)
    n = v.shape[0] // 128
    return np.ascontiguousarray(v.reshape(n, 128).T)


def host_prep_consts():
    c = {}
    c["h128"] = _hadamard(128).astype(bfnp)
    c["h16i8"] = np.kron(_hadamard(16), np.eye(8)).astype(bfnp)
    c["h64i2"] = np.kron(_hadamard(64), np.eye(2)).astype(bfnp)
    c["ident_bf"] = np.eye(128, dtype=bfnp)
    c["ident_f32"] = np.eye(128, dtype=np.float32)
    c["ones_bf"] = np.ones((128, 1), dtype=bfnp)
    kl = np.arange(128)[:, None]
    ql = np.arange(512)[None, :]
    m = np.concatenate(
        [(128 * rel + kl <= ql).astype(np.float32) for rel in range(4)], axis=1)
    c["cmask"] = m.astype(bfnp)
    dh = 64
    pos = np.arange(S, dtype=np.float64)
    inv = 1.0 / (10000.0 ** (np.arange(0, dh, 2, dtype=np.float64) / dh))
    fr = pos[:, None] * inv[None, :]
    cos = np.concatenate([np.cos(fr), np.cos(fr)], 1).astype(np.float32)
    sin = np.concatenate([np.sin(fr), np.sin(fr)], 1).astype(np.float32)
    c["cos_rep"] = np.ascontiguousarray(np.tile(cos.T, (2, 1)))
    c["sin_rep"] = np.ascontiguousarray(np.tile(sin.T, (2, 1)))
    R = np.zeros((64, 64), np.float32)
    for d_ in range(32):
        R[d_, d_ + 32] = -1.0
        R[d_ + 32, d_] = 1.0
    c["ropeRT"] = np.kron(np.eye(2), R.T).astype(bfnp)
    vals = [2.0, -1.0, 1.0, 0.5, 1e-5, 1e-6,
            1.0 / math.sqrt(128.0), 0.125, 64.0,
            1.0 / (SCALE1 * math.sqrt(2048.0)), 1.0 / (SCALE1 * math.sqrt(8192.0)),
            1.0 / 2048.0, 0.0]
    c["_pool_vals"] = vals
    c["cpool"] = np.ascontiguousarray(
        np.tile(np.array(vals, np.float32)[None, :], (128, 1)))
    return c


def host_prep_weights(inputs, r):
    d = {}
    gp = inputs["L0_ffn_gate_up_packed"]
    gs = inputs["L0_ffn_gate_up_scales"]
    dp = inputs["L0_ffn_down_packed"]
    ds = inputs["L0_ffn_down_scales"]
    d["gp"] = np.ascontiguousarray(gp[1408 * r:1408 * (r + 1)])
    d["gs"] = np.ascontiguousarray(gs.T[1408 * r:1408 * (r + 1)])
    d["dp"] = np.ascontiguousarray(dp[256 * r:256 * (r + 1)])
    d["ds"] = np.ascontiguousarray(ds.T[256 * r:256 * (r + 1)])
    wu = inputs["L1_ffn_W_up"]
    wg = inputs["L1_ffn_W_gate"]
    wd = inputs["L1_ffn_W_down"]
    d["wup"] = np.ascontiguousarray(wu[1024 * r:1024 * (r + 1)])
    d["wgate"] = np.ascontiguousarray(wg[1024 * r:1024 * (r + 1)])
    d["wdn1"] = np.ascontiguousarray(wd[256 * r:256 * (r + 1)])
    su_ug = inputs["L1_ffn_SU_ug"]
    su_dp = inputs["L1_ffn_SU_dp"]
    p = np.arange(128)
    g16 = np.arange(16)
    d["su_ug_h"] = np.ascontiguousarray(
        su_ug[128 * (p[:, None] // 8) + 8 * g16[None, :] + (p[:, None] % 8)]
    ).astype(np.float32)
    g64 = np.arange(64)
    d["su_dp_h"] = np.ascontiguousarray(
        su_dp[128 * (p[:, None] // 2) + 2 * g64[None, :] + (p[:, None] % 2)]
    ).astype(np.float32)
    return d


def emit_prep(b, io, parts=("dq", "fa", "fb"), mid_cb=None):
    nc = b.nc
    st = b._stack
    sb = lambda n, s, d: st.enter_context(nc.sbuf_tensor(n, s, d))
    ps = lambda n, s, d: st.enter_context(nc.psum_tensor(n, s, d))

    t_h128 = sb("t_h128", [128, 128], BF16)
    t_h16i8 = sb("t_h16i8", [128, 128], BF16)
    t_h64i2 = sb("t_h64i2", [128, 128], BF16)
    t_idbf = sb("t_idbf", [128, 128], BF16)
    t_suug = sb("t_suug", [128, 16], F32)
    t_sudp = sb("t_sudp", [128, 64], F32)
    last_out = []

    b.op("sync", lambda e: e.dma_start(out=t_h128[:], in_=io["h128"]),
         writes=["t_h128"], dma_chan="cst0")
    b.op("sync", lambda e: e.dma_start(out=t_h16i8[:], in_=io["h16i8"]),
         writes=["t_h16i8"], dma_chan="cst1")
    b.op("sync", lambda e: e.dma_start(out=t_h64i2[:], in_=io["h64i2"]),
         writes=["t_h64i2"], dma_chan="cst2")
    b.op("sync", lambda e: e.dma_start(out=t_idbf[:], in_=io["ident_bf"]),
         writes=["t_idbf"], dma_chan="cst3")
    b.op("sync", lambda e: e.dma_start(out=t_suug[:], in_=io["su_ug_h"]),
         writes=["t_suug"], dma_chan="cst4")
    b.op("sync", lambda e: e.dma_start(out=t_sudp[:], in_=io["su_dp_h"]),
         writes=["t_sudp"], dma_chan="cst5")
    b.op("vector", lambda e: e.tensor_scalar(
        out=t_suug[:], in0=t_suug[:], scalar1=1.0 / (SCALE1 * math.sqrt(2048.0)),
        scalar2=None, op0=ALU.mult), reads=["t_suug"], writes=["t_suug"])
    b.op("vector", lambda e: e.tensor_scalar(
        out=t_sudp[:], in0=t_sudp[:], scalar1=1.0 / (SCALE1 * math.sqrt(8192.0)),
        scalar2=None, op0=ALU.mult), reads=["t_sudp"], writes=["t_sudp"])

    from contextlib import ExitStack
    pbt = getattr(b, "_cx_pbt", None)
    pbx0 = getattr(b, "_cx_pb", None)
    if pbt is None:
        pbt = [ps("pbt0", [128, 1024], BF16), ps("pbt1", [128, 1024], BF16)]
    tb_ap = [pbt[0].ap()[:, 0:128], pbt[1].ap()[:, 0:128]]
    tb_reg = ["pbt0", "pbt1"]
    if pbx0 is not None:
        tb_ap += [pbx0[2].ap().bitcast(BF16)[:, 0:128],
                  pbx0[3].ap().bitcast(BF16)[:, 0:128]]
        tb_reg += ["pb2", "pb3"]
    NTB = len(tb_ap)
    CH = 2816
    stDQ = ExitStack()
    sbq = lambda n, s, d: stDQ.enter_context(nc.sbuf_tensor(n, s, d))
    t_pk = sbq("t_pk", [128, 2 * 704], I32)
    t_scf = sbq("t_scf", [128, 44], F16)
    t_sc = sbq("t_sc", [128, 44], F32)
    t_nib = sbq("t_nib", [128, CH], I32)
    t_sgn = sbq("t_sgn", [128, CH], I32)
    t_2f = sbq("t_2f", [128, CH], F32)
    t_dq = sbq("t_dq", [128, CH], BF16)
    t_wsl = sbq("t_wsl", [128, 5632], BF16)

    import os
    _ng = int(os.environ.get("N_OT_GU", "11"))
    _nd = int(os.environ.get("N_OT_DN", "2"))
    for fam_name, pk_ap, sc_ap, n_ot, n_w, out_ap in ([
        ("gu", io["gp"], io["gs"], _ng, 256, io["wgu_out"]),
        ("dn", io["dp"], io["ds"], _nd, 704, io["wdn_out"]),
    ] if "dq" in parts else []):
        m_major = fam_name == "gu"
        n_i = n_w * 8
        n_ch = (n_i + CH - 1) // CH
        for ot in range(n_ot):
            slot = ot % 2
            pk_sl = t_pk.ap()[:, slot * 704: slot * 704 + n_w]
            pkreg = f"t_pk{slot}"
            b.op("sync", lambda e, a=pk_sl, s=pk_ap, o=ot, w=n_w: e.dma_start(
                out=a, in_=s[128 * o:128 * (o + 1), :]),
                writes=[pkreg], dma_chan=f"dqpk{slot}")
            import os as _os
            _stop = int(_os.environ.get("DQ_STOP", "8"))
            ng_t = n_i // 128
            if _stop < 2:
                continue
            b.op("sync", lambda e, s=sc_ap, o=ot, g=ng_t: e.dma_start(
                out=t_scf[:, 0:g], in_=s[128 * o:128 * (o + 1), :]),
                writes=["t_scf"], dma_chan="dqsc")
            b.op("vector", lambda e, g=ng_t: e.tensor_copy(
                out=t_sc[:, 0:g], in_=t_scf[:, 0:g]),
                reads=["t_scf"], writes=["t_sc"])
            for ch in (range(n_ch) if _stop >= 3 else []):
                i0 = ch * CH
                n_c = min(CH, n_i - i0)
                w0 = i0 // 8
                n_wc = n_c // 8
                for k in range(8):
                    b.op("vector", lambda e, a=pk_sl, k=k, w0=w0, nw=n_wc,
                         n=n_c: e.tensor_scalar(
                        out=t_nib.ap()[:, k:n:8], in0=a[:, w0:w0 + nw],
                        scalar1=4 * k, scalar2=15,
                        op0=ALU.logical_shift_right, op1=ALU.bitwise_and),
                        reads=[pkreg], writes=["t_nib"])
                if _stop < 4:
                    continue
                b.op("vector", lambda e, n=n_c: e.tensor_scalar(
                    out=t_sgn[:, 0:n], in0=t_nib[:, 0:n], scalar1=8, scalar2=28,
                    op0=ALU.bitwise_and, op1=ALU.logical_shift_left),
                    reads=["t_nib"], writes=["t_sgn"])
                b.op("vector", lambda e, n=n_c: e.tensor_scalar(
                    out=t_nib[:, 0:n], in0=t_nib[:, 0:n], scalar1=7, scalar2=22,
                    op0=ALU.bitwise_and, op1=ALU.logical_shift_left),
                    reads=["t_nib"], writes=["t_nib"])
                b.op("vector", lambda e, n=n_c: e.tensor_scalar(
                    out=t_nib[:, 0:n], in0=t_nib[:, 0:n], scalar1=0x3F000000,
                    scalar2=None, op0=ALU.add), reads=["t_nib"], writes=["t_nib"])
                if _stop < 5:
                    continue
                b.op("scalar", lambda e, n=n_c: e.mul(
                    out=t_2f[:, 0:n], in_=t_nib.ap().bitcast(F32)[:, 0:n],
                    mul=2.0), reads=["t_nib"], writes=["t_2f"])
                b.op("vector", lambda e, n=n_c: e.tensor_scalar(
                    out=t_2f[:, 0:n], in0=t_2f[:, 0:n], scalar1=-1.0,
                    scalar2=None, op0=ALU.add), reads=["t_2f"], writes=["t_2f"])
                b.op("vector", lambda e, n=n_c: e.tensor_tensor(
                    out=t_nib.ap().bitcast(F32)[:, 0:n],
                    in0=t_nib.ap().bitcast(F32)[:, 0:n], in1=t_2f[:, 0:n],
                    op=ALU.min), reads=["t_nib", "t_2f"], writes=["t_nib"])
                b.op("vector", lambda e, n=n_c: e.tensor_tensor(
                    out=t_nib[:, 0:n], in0=t_nib[:, 0:n], in1=t_sgn[:, 0:n],
                    op=ALU.bitwise_or), reads=["t_nib", "t_sgn"],
                    writes=["t_nib"])
                if _stop < 6:
                    continue
                for g in range(n_c // 128):
                    gg = i0 // 128 + g
                    b.op("vector", lambda e, g=g, gg=gg: e.tensor_scalar(
                        out=t_dq[:, 128 * g:128 * (g + 1)],
                        in0=t_nib.ap().bitcast(F32)[:, 128 * g:128 * (g + 1)],
                        scalar1=t_sc.ap()[:, gg:gg + 1], scalar2=None,
                        op0=ALU.mult),
                        reads=["t_nib", "t_sc"], writes=["t_dq"])
                if _stop < 7:
                    continue
                for k in range(n_c // 128):
                    kk = i0 // 128 + k
                    sl = k % NTB
                    preg = tb_reg[sl]
                    b.op("tensor", lambda e, k=k, sl=sl: e.transpose(
                        tb_ap[sl],
                        t_dq[:, 128 * k:128 * (k + 1)], t_idbf[:]),
                        reads=["t_dq", "t_idbf"], writes=[preg])
                    b.op("scalar", lambda e, kk=kk, sl=sl: e.copy(
                        out=t_wsl[:, 128 * kk:128 * (kk + 1)],
                        in_=tb_ap[sl]),
                        reads=[preg], writes=[f"t_wsl{kk%2}"])
            if _stop >= 8:
                tok = b.op("sync", lambda e, o=ot, a=out_ap, n=n_i,
                           mm=m_major: e.dma_start(
                    out=(a[:, o, :] if mm else a[:, :, 128 * o:128 * (o + 1)]),
                    in_=t_wsl[:, 0:n]),
                    reads=["t_wsl0", "t_wsl1"], dma_chan="dqout")
        if n_ot and _stop >= 8:
            last_out.append(tok)

    stDQ.close()
    b.fence()
    if mid_cb is not None:
        mid_cb()
    stFD = ExitStack()
    sbf = lambda n, s, d: stFD.enter_context(nc.sbuf_tensor(n, s, d))
    t_w = sbf("t_w", [128, 4096], F32)
    t_wb = sbf("t_wb", [128, 4096], BF16)
    wT_sb = sbf("wT_sb", [128, 16384], BF16)
    qk = lambda col0, ncol: [f"wTq{q}" for q in range(col0 // 256,
                                                      (col0 + ncol) // 256)]
    pbx = getattr(b, "_cx_pb", None)
    if pbx is None:
        pb0 = ps("pb0", [128, 512], F32)
        pb1 = ps("pb1", [128, 512], F32)
    else:
        pb0, pb1 = pbx[0], pbx[1]
    t_y2 = sbf("t_y2", [128, 2560], BF16)
    t_o = sbf("t_o", [128, 5120], BF16)

    for part, (src, y1s, m_base) in (enumerate([
        (io["wup"], io["y1a"], 0), (io["wgate"], io["y1b"], 8)])
            if "fa" in parts else []):
        for ot in range(8):
            b.op("sync", lambda e, s=src, o=ot: e.dma_start(
                out=t_w[:, 0:2048], in_=s[128 * o:128 * (o + 1), :]),
                writes=["t_w"], dma_chan="fw")
            b.op("vector", lambda e: e.tensor_copy(
                out=t_wb[:, 0:2048], in_=t_w[:, 0:2048]),
                reads=["t_w"], writes=["t_wb"])
            for k in range(16):
                sl = k % NTB
                b.op("tensor", lambda e, k=k, sl=sl: e.transpose(
                    tb_ap[sl],
                    t_wb[:, 128 * k:128 * (k + 1)], t_idbf[:]),
                    reads=["t_wb", "t_idbf"], writes=[tb_reg[sl]])
                col0 = 1024 * k + 128 * ot
                b.op("scalar", lambda e, c=col0, sl=sl: e.copy(
                    out=wT_sb[:, c:c + 128],
                    in_=tb_ap[sl]),
                    reads=[tb_reg[sl]], writes=qk(col0 - col0 % 256, 256))
        for A in range(16):
            for c in range(2):
                pbsl = (A * 2 + c) % 2
                pb = [pb0, pb1][pbsl]
                b.op("tensor", lambda e, A=A, c=c, pb=pb: e.matmul(
                    pb[:], t_h128[:],
                    wT_sb[:, 1024 * A + 512 * c:1024 * A + 512 * (c + 1)],
                    start=True, stop=True),
                    reads=qk(1024 * A + 512 * c, 512) + ["t_h128"],
                    writes=[f"pb{pbsl}"]),
                ap0 = 2048 * (A % 2)
                b.op("scalar", lambda e, c=c, pb=pb, ap0=ap0: e.copy(
                    out=t_o[:, ap0 + 512 * c:ap0 + 512 * (c + 1)], in_=pb[:]),
                    reads=[f"pb{pbsl}"], writes=[f"t_o{A%2}{c}"])
            b.op("sync", lambda e, A=A, y=y1s, ap0=2048 * (A % 2): e.dma_start(
                out=y[A], in_=t_o[:, ap0:ap0 + 1024]),
                reads=[f"t_o{A%2}0", f"t_o{A%2}1"], writes=[f"y1_{part}"],
                dma_chan=f"fy1{part}{A%2}")
        for g in range(16):
            gp = g % 2
            b.op("sync", lambda e, g=g, y=y1s, gp=gp: e.dma_start(
                out=t_y2[:, 1024 * gp:1024 * (gp + 1)],
                in_=y[:, 8 * g:8 * (g + 1), :]),
                reads=[f"y1_{part}"], writes=[f"t_y2{gp}"],
                dma_chan=f"fy2{part}{gp}")
            for c in range(2):
                pb = [pb0, pb1][c]
                b.op("tensor", lambda e, c=c, pb=pb, gp=gp: e.matmul(
                    pb[:], t_h16i8[:],
                    t_y2[:, 1024 * gp + 512 * c:1024 * gp + 512 * (c + 1)],
                    start=True, stop=True),
                    reads=[f"t_y2{gp}", "t_h16i8"], writes=[f"pb{c}"])
                b.op("scalar", lambda e, c=c, g=g, pb=pb, gp=gp: e.activation(
                    out=t_o[:, 2048 * gp + 512 * c:2048 * gp + 512 * (c + 1)],
                    in_=pb[:],
                    func=AF.Identity, scale=t_suug.ap()[:, g:g + 1]),
                    reads=[f"pb{c}", "t_suug"], writes=[f"t_o{gp}{c}"])

            for c in range(2):
                for mm in range(4):
                    def mk_scatter(g=g, m_base=m_base, c=c, mm=mm, gp=gp):
                        def fn(e):
                            out3 = io["wug_out"]
                            m_loc = m_base + 4 * c + mm
                            view = out3[8 * g:8 * (g + 1), m_loc, :]
                            view = view.rearrange("b (k f) -> k b f", f=128)
                            return e.dma_start(
                                out=view,
                                in_=t_o[:, 2048 * gp + 512 * c + 128 * mm:
                                        2048 * gp + 512 * c + 128 * (mm + 1)])
                        return fn
                    tok = b.op("sync", mk_scatter(), reads=[f"t_o{gp}{c}"],
                               dma_chan=f"fout{gp}{c}{mm % 2}")
        last_out.append(tok)

    for ot in (range(2) if "fb" in parts else []):
        for hf in range(2):
            b.op("sync", lambda e, o=ot, h=hf: e.dma_start(
                out=t_w[:], in_=io["wdn1"][128 * o:128 * (o + 1),
                                           4096 * h:4096 * (h + 1)]),
                writes=["t_w"], dma_chan="fw")
            b.op("vector", lambda e: e.tensor_copy(out=t_wb[:], in_=t_w[:]),
                 reads=["t_w"], writes=["t_wb"])
            for kloc in range(32):
                k = 32 * hf + kloc
                sl = kloc % NTB
                b.op("tensor", lambda e, kloc=kloc, sl=sl: e.transpose(
                    tb_ap[sl],
                    t_wb[:, 128 * kloc:128 * (kloc + 1)], t_idbf[:]),
                    reads=["t_wb", "t_idbf"], writes=[tb_reg[sl]])
                col0 = 256 * k + 128 * ot
                b.op("scalar", lambda e, c=col0, sl=sl: e.copy(
                    out=wT_sb[:, c:c + 128],
                    in_=tb_ap[sl]),
                    reads=[tb_reg[sl]], writes=qk(col0 - col0 % 256, 256))
    for A in (range(64) if "fb" in parts else []):
        pb = [pb0, pb1][A % 2]
        b.op("tensor", lambda e, A=A, pb=pb: e.matmul(
            pb[:, 0:256], t_h128[:], wT_sb[:, 256 * A:256 * (A + 1)],
            start=True, stop=True),
            reads=qk(256 * A, 256) + ["t_h128"], writes=[f"pb{A%2}"])
        sB = A % 4
        b.op("scalar", lambda e, A=A, pb=pb, sB=sB: e.copy(
            out=t_o[:, 4096 + 256 * sB:4096 + 256 * (sB + 1)],
            in_=pb[:, 0:256]),
            reads=[f"pb{A%2}"], writes=[f"t_oB{sB}"])
        b.op("sync", lambda e, A=A, sB=sB: e.dma_start(
            out=io["y1d"][A], in_=t_o[:, 4096 + 256 * sB:4096 + 256 * (sB + 1)]),
            reads=[f"t_oB{sB}"], writes=["y1_d"], dma_chan=f"fy1d{sB}")
    for g in (range(64) if "fb" in parts else []):
        gp = g % 2
        gq = g % 4
        b.op("sync", lambda e, g=g, gp=gp: e.dma_start(
            out=t_y2[:, 2048 + 256 * gp:2048 + 256 * (gp + 1)],
            in_=io["y1d"][:, 2 * g:2 * (g + 1), :]),
            reads=["y1_d"], writes=[f"t_y2d{gp}"], dma_chan=f"fy2d{gp}")
        pbd = [pb0, pb1][gp]
        b.op("tensor", lambda e, gp=gp, pbd=pbd: e.matmul(
            pbd[:, 0:256], t_h64i2[:],
            t_y2[:, 2048 + 256 * gp:2048 + 256 * (gp + 1)],
            start=True, stop=True),
            reads=[f"t_y2d{gp}", "t_h64i2"], writes=[f"pb{gp}"])
        b.op("scalar", lambda e, g=g, gq=gq, pbd=pbd: e.activation(
            out=t_o[:, 4096 + 256 * gq:4096 + 256 * (gq + 1)],
            in_=pbd[:, 0:256],
            func=AF.Identity,
            scale=t_sudp.ap()[:, g:g + 1]),
            reads=[f"pb{gp}", "t_sudp"], writes=[f"t_oB{gq}"])

        def mk_scat_d(g=g, gq=gq):
            def fn(e):
                out3 = io["wdn1_out"]
                view = out3[2 * g:2 * (g + 1), :, :]
                view = view.transpose([1, 0, 2])
                return e.dma_start(
                    out=view, in_=t_o[:, 4096 + 256 * gq:4096 + 256 * (gq + 1)])
            return fn
        tok = b.op("sync", mk_scat_d(), reads=[f"t_oB{gq}"],
                   dma_chan=f"foutd{gq}")
        last_fb = tok
    if "fb" in parts:
        last_out.append(last_fb)
    stFD.close()
    return [("sync", t) for t in last_out]


def build_prep_nc(parts=("dq", "fa", "fb")):
    nc = bass.Bass("TRN2", target_bir_lowering=False)
    io = {}

    def di(name, shape, dt):
        io[name] = nc.dram_tensor(name, list(shape), dt, kind="ExternalInput").ap()

    def do(name, shape, dt):
        io[name] = nc.dram_tensor(name, list(shape), dt, kind="ExternalOutput").ap()

    di("h128", [128, 128], BF16)
    di("h16i8", [128, 128], BF16)
    di("h64i2", [128, 128], BF16)
    di("ident_bf", [128, 128], BF16)
    di("su_ug_h", [128, 16], F32)
    di("su_dp_h", [128, 64], F32)
    di("gp", [1408, 256], I32)
    di("gs", [1408, 16], F16)
    di("dp", [256, 704], I32)
    di("ds", [256, 44], F16)
    di("wup", [1024, 2048], F32)
    di("wgate", [1024, 2048], F32)
    di("wdn1", [256, 8192], F32)
    do("wgu_out", [128, 11, 2048], BF16)
    do("wdn_out", [128, 44, 256], BF16)
    do("wug_out", [128, 16, 2048], BF16)
    do("wdn1_out", [128, 64, 256], BF16)
    io["y1a"] = nc.dram_tensor("y1a", [16, 128, 1024], BF16).ap()
    io["y1b"] = nc.dram_tensor("y1b", [16, 128, 1024], BF16).ap()
    io["y1d"] = nc.dram_tensor("y1d", [64, 128, 256], BF16).ap()
    make_program(nc, lambda b: emit_prep(b, io, parts))
    return nc


POOL_VALS = [2.0, -1.0, 1.0, 0.5, 1e-5, 1e-6,
             1.0 / math.sqrt(128.0), 0.125, 64.0,
             1.0 / (SCALE1 * math.sqrt(2048.0)), 1.0 / (SCALE1 * math.sqrt(8192.0)),
             1.0 / 2048.0, 0.0]


class Ctx:
    def __init__(self, b, io, Tt, alloc_stream=True, alloc_tokln=False):
        nc = b.nc
        st = b._stack
        self.b = b
        self.io = io
        self.T = Tt
        self.NC = Tt // 512
        sb = lambda n, s, d: st.enter_context(nc.sbuf_tensor(n, s, d))
        ps = lambda n, s, d: st.enter_context(nc.psum_tensor(n, s, d))
        if alloc_stream:
            self.xres = sb("xres", [128, NDT * Tt], F32)
            self.xln = sb("xln", [128, NDT * Tt], BF16)
        self.rows = sb("rows", [1, 4 * Tt], F32)
        self.xio = sb("s_xio", [128, 2048], F32)
        self.t_ab = sb("t_ab", [128, 512], F32)
        self.t_bb = sb("t_bb", [128, 512], F32)
        self.t_tmp = sb("t_tmp", [128, 2 * 512], F32)
        self.t_sq = sb("t_sq", [128, 2 * 512], BF16)
        self.t_xb = sb("t_xb", [128, 2 * 512], BF16)
        if alloc_tokln:
            self.t_xn = sb("t_xn", [128, 2048], BF16)
            self.t_sqf = sb("t_sqf", [128, 2048], F32)
            self.t_mr = sb("t_mr", [128, 4], F32)
        self.cpool = sb("s_cpool", [128, len(POOL_VALS)], F32)
        self.ones_bf = sb("s_ones_bf", [128, 1], BF16)
        self.onesrow = sb("s_onesrow", [1, 128], F32)
        self.id_f32 = sb("s_id_f32", [128, 128], F32)
        self.pb = [ps(f"pb{i}", [128, 512], F32) for i in range(6)]
        self.pbt = [ps("pbt0", [128, 1024], BF16), ps("pbt1", [128, 1024], BF16)]
        b.op("gpsimd", lambda e: e.dma_start(out=self.cpool[:], in_=io["cpool"]),
             writes=["cpool"], dma_chan="c_cp")
        b.op("gpsimd", lambda e: e.dma_start(out=self.ones_bf[:], in_=io["ones_bf"]),
             writes=["ones_bf"], dma_chan="c_ob")
        b.op("gpsimd", lambda e: e.dma_start(out=self.onesrow[:], in_=io["onesrow"]),
             writes=["onesrow"], dma_chan="c_or")
        b.op("gpsimd", lambda e: e.dma_start(out=self.id_f32[:], in_=io["ident_f32"]),
             writes=["id_f32"], dma_chan="c_id")
        for i, v in enumerate(POOL_VALS):
            nc.const_aps.register(F32, v, self.cpool.ap()[:, i:i + 1])
        self._cv = {v: i for i, v in enumerate(POOL_VALS)}

    def cv(self, v):
        return self.cpool.ap()[:, self._cv[v]:self._cv[v] + 1]

    def cv1(self, v):
        return self.cpool.ap()[0:1, self._cv[v]:self._cv[v] + 1]

    def x(self, i, c=None):
        a = self.xres.ap()[:, i * self.T:(i + 1) * self.T]
        return a if c is None else a[:, 512 * c:512 * (c + 1)]

    def ln(self, i, c=None):
        a = self.xln.ap()[:, i * self.T:(i + 1) * self.T]
        return a if c is None else a[:, 512 * c:512 * (c + 1)]


def emit_loadx(b, cx, x_src):
    nc = b.nc
    st = b._stack
    t_in = cx.xio
    for tt in range(cx.T // 128):
        reg = "t_xin0"
        b.op("gpsimd", lambda e, tt=tt: e.dma_start(
            out=t_in[:, 0:2048],
            in_=x_src[128 * tt:128 * (tt + 1), :]),
            writes=[reg], dma_chan="xin0")
        for i in range(NDT):
            pbi = i % 4
            b.op("tensor", lambda e, i=i, pbi=pbi: e.transpose(
                cx.pb[pbi].ap()[:, 0:128],
                t_in[:, 128 * i:128 * (i + 1)],
                cx.id_f32[:]),
                reads=[reg, "id_f32"], writes=[f"pb{pbi}"])
            b.op("scalar", lambda e, i=i, tt=tt, pbi=pbi: e.copy(
                out=cx.x(i)[:, 128 * tt:128 * (tt + 1)],
                in_=cx.pb[pbi].ap()[:, 0:128]),
                reads=[f"pb{pbi}"], writes=[f"x{i}"])


def emit_storex(b, cx, x_dst):
    nc = b.nc
    st = b._stack
    t_out = cx.xio
    toks = []
    for tt in range(cx.T // 128):
        reg = "t_xout0"
        for i in range(NDT):
            pbi = 4 + (i % 2)
            b.op("tensor", lambda e, i=i, tt=tt, pbi=pbi: e.transpose(
                cx.pb[pbi].ap()[:, 0:128],
                cx.x(i)[:, 128 * tt:128 * (tt + 1)], cx.id_f32[:]),
                reads=[f"x{i}", "id_f32"], writes=[f"pb{pbi}"])
            b.op("scalar", lambda e, i=i, pbi=pbi: e.copy(
                out=t_out[:, 128 * i:128 * (i + 1)],
                in_=cx.pb[pbi].ap()[:, 0:128]),
                reads=[f"pb{pbi}"], writes=[reg])
        toks.append(b.op("gpsimd", lambda e, tt=tt: e.dma_start(
            out=x_dst[128 * tt:128 * (tt + 1), :],
            in_=t_out[:, 0:2048]),
            reads=[reg], dma_chan="xout0"))
    return toks


def emit_norm_rows(b, cx, src_fn, n_tiles, eps, with_mean, out_scale_const,
                   tag, chunks):
    T_ = cx.T
    R = cx.rows.ap()
    r_sum = R[:, 0 * T_:1 * T_]
    r_sq = R[:, 1 * T_:2 * T_]
    r_a = R[:, 2 * T_:3 * T_]
    r_b = R[:, 3 * T_:4 * T_]
    nrm = 1.0 / (128.0 * n_tiles)
    for c in chunks:
        for i in range(n_tiles):
            s0, s1 = (i % 2) * 512, (i % 2) * 512 + 512
            if with_mean:
                b.op("vector", lambda e, i=i, c=c, s0=s0, s1=s1: e.tensor_copy(
                    out=cx.t_xb[:, s0:s1], in_=src_fn(i)[:, 512 * c:512 * (c + 1)]),
                    reads=[f"{tag}{i}"], writes=[f"t_xb{i%2}"])
            b.op("scalar", lambda e, i=i, c=c, s0=s0, s1=s1: e.activation(
                out=cx.t_sq[:, s0:s1], in_=src_fn(i)[:, 512 * c:512 * (c + 1)],
                func=AF.Square),
                reads=[f"{tag}{i}"], writes=[f"t_sq{i%2}"])
            if with_mean:
                b.op("tensor", lambda e, i=i, s0=s0, s1=s1: e.matmul(
                    cx.pb[4].ap()[0:1, :], cx.ones_bf[:], cx.t_xb[:, s0:s1],
                    start=(i == 0), stop=(i == n_tiles - 1)),
                    reads=[f"t_xb{i%2}", "ones_bf"], writes=["pb4"])
            b.op("tensor", lambda e, i=i, s0=s0, s1=s1: e.matmul(
                cx.pb[5].ap()[0:1, :], cx.ones_bf[:], cx.t_sq[:, s0:s1],
                start=(i == 0), stop=(i == n_tiles - 1)),
                reads=[f"t_sq{i%2}", "ones_bf"], writes=["pb5"])
        slc = lambda r, c: r[:, 512 * c:512 * (c + 1)]
        if with_mean:
            b.op("scalar", lambda e, c=c: e.copy(
                out=r_sum[:, 512 * c:512 * (c + 1)], in_=cx.pb[4].ap()[0:1, :]),
                reads=["pb4"], writes=["r_sum"])
        b.op("scalar", lambda e, c=c: e.copy(
            out=r_sq[:, 512 * c:512 * (c + 1)], in_=cx.pb[5].ap()[0:1, :]),
            reads=["pb5"], writes=["r_sq"])
        b.op("vector", lambda e, c=c: e.tensor_scalar(
            out=slc(r_sq, c), in0=slc(r_sq, c), scalar1=nrm, scalar2=None,
            op0=ALU.mult), reads=["r_sq"], writes=["r_sq"])
        if with_mean:
            b.op("vector", lambda e, c=c: e.tensor_scalar(
                out=slc(r_sum, c), in0=slc(r_sum, c), scalar1=nrm, scalar2=None,
                op0=ALU.mult), reads=["r_sum"], writes=["r_sum"])
            b.op("vector", lambda e, c=c: e.tensor_tensor(
                out=slc(r_a, c), in0=slc(r_sum, c), in1=slc(r_sum, c),
                op=ALU.mult), reads=["r_sum"], writes=["r_a"])
            b.op("vector", lambda e, c=c: e.tensor_tensor(
                out=slc(r_sq, c), in0=slc(r_sq, c), in1=slc(r_a, c),
                op=ALU.subtract), reads=["r_sq", "r_a"], writes=["r_sq"])
        b.op("scalar", lambda e, c=c: e.activation(
            out=slc(r_a, c), in_=slc(r_sq, c), func=AF.Sqrt,
            bias=cx.cpool.ap()[0:1, cx._cv[eps]:cx._cv[eps] + 1]),
            reads=["r_sq", "cpool"], writes=["r_a"])
        b.op("scalar", lambda e, c=c: e.activation(
            out=slc(r_a, c), in_=slc(r_a, c), func=AF.Ln),
            reads=["r_a", "cpool"], writes=["r_a"])
        b.op("scalar", lambda e, c=c: e.activation(
            out=slc(r_a, c), in_=slc(r_a, c), func=AF.Exp,
            scale=cx.cv1(-1.0)), reads=["r_a", "cpool"], writes=["r_a"])
        if out_scale_const != 1.0:
            b.op("vector", lambda e, c=c: e.tensor_scalar(
                out=slc(r_a, c), in0=slc(r_a, c), scalar1=out_scale_const,
                scalar2=None, op0=ALU.mult), reads=["r_a"], writes=["r_a"])
        if with_mean:
            b.op("vector", lambda e, c=c: e.tensor_tensor(
                out=slc(r_b, c), in0=slc(r_sum, c), in1=slc(r_a, c),
                op=ALU.mult), reads=["r_sum", "r_a"], writes=["r_b"])
            b.op("vector", lambda e, c=c: e.tensor_scalar(
                out=slc(r_b, c), in0=slc(r_b, c), scalar1=-1.0, scalar2=None,
                op0=ALU.mult), reads=["r_b"], writes=["r_b"])
    return r_a, (r_b if with_mean else None)


def emit_bcast_rows(b, cx, r_a, r_b, c):
    b.op("tensor", lambda e, c=c: e.matmul(
        cx.pb[4][:], cx.onesrow[:], r_a[:, 512 * c:512 * (c + 1)],
        start=True, stop=True),
        reads=["r_a", "onesrow"], writes=["pb4"])
    b.op("scalar", lambda e: e.copy(out=cx.t_ab[:], in_=cx.pb[4][:]),
         reads=["pb4"], writes=["t_ab"])
    if r_b is not None:
        b.op("tensor", lambda e, c=c: e.matmul(
            cx.pb[5][:], cx.onesrow[:], r_b[:, 512 * c:512 * (c + 1)],
            start=True, stop=True),
            reads=["r_b", "onesrow"], writes=["pb5"])
        b.op("scalar", lambda e: e.copy(out=cx.t_bb[:], in_=cx.pb[5][:]),
             reads=["pb5"], writes=["t_bb"])


def emit_ln(b, cx):
    r_a, r_b = emit_norm_rows(b, cx, cx.x, NDT, 1e-5, True, 1.0, "x",
                              list(range(cx.NC)))
    for c in range(cx.NC):
        emit_bcast_rows(b, cx, r_a, r_b, c)
        for i in range(NDT):
            sl = i % 2
            b.op("vector", lambda e, i=i, c=c, sl=sl: e.tensor_tensor(
                out=cx.t_tmp[:, 512 * sl:512 * (sl + 1)],
                in0=cx.x(i, c), in1=cx.t_ab[:], op=ALU.mult),
                reads=[f"x{i}", "t_ab"], writes=[f"t_tmp{sl}"])
            b.op("gpsimd", lambda e, i=i, c=c, sl=sl: e.tensor_tensor(
                out=cx.ln(i, c), in0=cx.t_tmp[:, 512 * sl:512 * (sl + 1)],
                in1=cx.t_bb[:], op=ALU.add),
                reads=[f"t_tmp{sl}", "t_bb"], writes=[f"xln{i}"])


def emit_ffn0(b, cx, io, tag="f0"):
    nc = b.nc
    from contextlib import ExitStack
    stF = ExitStack()
    T_ = cx.T
    sb = lambda n, s, d: stF.enter_context(nc.sbuf_tensor(n, s, d))
    wbuf = sb(f"{tag}_wbuf", [128, 4 * 2048], BF16)
    wdbuf = sb(f"{tag}_wdbuf", [128, 2 * 11 * 256], BF16)
    t_g = sb(f"{tag}_g", [128, 512], BF16)
    t_u = sb(f"{tag}_u", [128, 512], BF16)
    h = sb(f"{tag}_h", [128, 11 * 512], BF16)
    y = sb(f"{tag}_y", [128, NDT * 512], F32)
    wgu_tile, wdn_tile = io["wgu_tile"], io["wdn_tile"]
    wrd = io.get("w_regions", [])

    for tc in range(cx.NC):
        for oc in range(4):
            for j in range(11):
                m_g = 11 * oc + j
                m_u = 44 + m_g
                sl_g, sl_u = 2 * (j % 2), 2 * (j % 2) + 1
                for qi, (m, sl) in enumerate(((m_g, sl_g), (m_u, sl_u))):
                    b.op("sync" if qi == 0 else "scalar",
                         lambda e, m=m, sl=sl: e.dma_start(
                        out=wbuf.ap().rearrange("p (s k f) -> p s k f", s=4,
                                                f=128)[:, sl, :, :],
                        in_=wgu_tile(m)),
                        reads=wrd, writes=[f"f0w{sl}"], dma_chan=f"f0w{sl}")
                pg, pu = (j % 2), 2 + (j % 2)
                for k in range(16):
                    b.op("tensor", lambda e, k=k, sl=sl_g, tc=tc,
                         pg=pg: e.matmul(
                        cx.pb[pg][:],
                        wbuf[:, 2048 * sl + 128 * k:2048 * sl + 128 * (k + 1)],
                        cx.ln(k, tc), start=(k == 0), stop=(k == 15)),
                        reads=[f"f0w{sl_g}", f"xln{k}"], writes=[f"pb{pg}"])
                for k in range(16):
                    b.op("tensor", lambda e, k=k, sl=sl_u, tc=tc,
                         pu=pu: e.matmul(
                        cx.pb[pu][:],
                        wbuf[:, 2048 * sl + 128 * k:2048 * sl + 128 * (k + 1)],
                        cx.ln(k, tc), start=(k == 0), stop=(k == 15)),
                        reads=[f"f0w{sl_u}", f"xln{k}"], writes=[f"pb{pu}"])
                b.op("scalar", lambda e, pg=pg: e.activation(
                    out=t_g[:], in_=cx.pb[pg][:], func=AF.Silu),
                    reads=[f"pb{pg}"], writes=["f0_g"])
                b.op("scalar", lambda e, pu=pu: e.copy(
                    out=t_u[:], in_=cx.pb[pu][:]),
                    reads=[f"pb{pu}"], writes=["f0_u"])
                b.op("vector", lambda e, j=j: e.tensor_tensor(
                    out=h[:, 512 * j:512 * (j + 1)],
                    in0=t_g[:], in1=t_u[:], op=ALU.mult),
                    reads=["f0_g", "f0_u"], writes=[f"f0h{j}"])
            for j2 in range(8):
                sl = j2 % 2
                b.op("sync", lambda e, oc=oc, j2=j2, sl=sl: e.dma_start(
                    out=wdbuf.ap().rearrange("p (s k f) -> p s k f", s=2,
                                             f=256)[:, sl, :, :],
                    in_=wdn_tile(oc, j2)),
                    reads=wrd, writes=[f"f0d{sl}"], dma_chan=f"f0d{sl}")
                for dt in range(2):
                    i_out = 2 * j2 + dt
                    pbi = 4 + dt
                    for k in range(11):
                        b.op("tensor", lambda e, k=k, sl=sl, dt=dt,
                             pbi=pbi: e.matmul(
                            cx.pb[pbi][:],
                            wdbuf[:, 2816 * sl + 256 * k + 128 * dt:
                                  2816 * sl + 256 * k + 128 * (dt + 1)],
                            h[:, 512 * k:512 * (k + 1)],
                            start=(k == 0), stop=(k == 10)),
                            reads=[f"f0d{sl}", f"f0h{k}"], writes=[f"pb{pbi}"])
                    ysl = y.ap()[:, 512 * i_out:512 * (i_out + 1)]
                    if oc == 0:
                        b.op("scalar", lambda e, a=ysl, pbi=pbi: e.copy(
                            out=a, in_=cx.pb[pbi][:]),
                            reads=[f"pb{pbi}"], writes=[f"f0y{i_out}"])
                    else:
                        b.op("vector", lambda e, a=ysl, pbi=pbi: e.tensor_tensor(
                            out=a, in0=a, in1=cx.pb[pbi][:], op=ALU.add),
                            reads=[f"pb{pbi}", f"f0y{i_out}"],
                            writes=[f"f0y{i_out}"])
        yfn = lambda i: y.ap()[:, 512 * i:512 * (i + 1)]
        r_a, _ = emit_norm_rows(b, cx, yfn, NDT, 1e-6, False, 0.5, "f0y", [0])
        emit_bcast_rows(b, cx, r_a, None, 0)
        for i in range(NDT):
            sl = i % 2
            b.op("vector", lambda e, i=i, sl=sl: e.tensor_tensor(
                out=cx.t_tmp[:, 512 * sl:512 * (sl + 1)],
                in0=yfn(i), in1=cx.t_ab[:], op=ALU.mult),
                reads=[f"f0y{i}", "t_ab"], writes=[f"t_tmp{sl}"])
            b.op("gpsimd", lambda e, i=i, tc=tc, sl=sl: e.tensor_tensor(
                out=cx.x(i, tc), in0=cx.x(i, tc),
                in1=cx.t_tmp[:, 512 * sl:512 * (sl + 1)], op=ALU.add),
                reads=[f"t_tmp{sl}", f"x{i}"], writes=[f"x{i}"])
    stF.close()


def emit_tokln_transpose(b, cx, x_src, n_tiles, dst, dst_stride, tag):
    nc = b.nc
    st = b._stack
    xio, t_xn, t_sqf, t_mr = cx.xio, cx.t_xn, cx.t_sqf, cx.t_mr
    for tt in range(n_tiles):
        b.op("gpsimd", lambda e, tt=tt: e.dma_start(
            out=xio[:], in_=x_src[128 * tt:128 * (tt + 1), :]),
            writes=["xio"], dma_chan="tln_in")
        b.op("vector", lambda e: e.reduce_sum(
            out=t_mr[:, 0:1], in_=xio[:], axis=AX.X),
            reads=["xio"], writes=["t_mr"])
        b.op("scalar", lambda e: e.activation(
            out=t_sqf[:], in_=xio[:], func=AF.Square),
            reads=["xio"], writes=["t_sqf"])
        b.op("vector", lambda e: e.reduce_sum(
            out=t_mr[:, 1:2], in_=t_sqf[:], axis=AX.X),
            reads=["t_sqf"], writes=["t_mr"])
        b.op("vector", lambda e: e.tensor_scalar(
            out=t_mr[:, 0:2], in0=t_mr[:, 0:2], scalar1=1.0 / 2048.0,
            scalar2=None, op0=ALU.mult), reads=["t_mr"], writes=["t_mr"])
        b.op("vector", lambda e: e.tensor_tensor(
            out=t_mr[:, 2:3], in0=t_mr[:, 0:1], in1=t_mr[:, 0:1], op=ALU.mult),
            reads=["t_mr"], writes=["t_mr"])
        b.op("vector", lambda e: e.tensor_tensor(
            out=t_mr[:, 1:2], in0=t_mr[:, 1:2], in1=t_mr[:, 2:3],
            op=ALU.subtract), reads=["t_mr"], writes=["t_mr"])
        b.op("scalar", lambda e: e.activation(
            out=t_mr[:, 1:2], in_=t_mr[:, 1:2], func=AF.Sqrt,
            bias=cx.cv(1e-5)), reads=["t_mr", "cpool"], writes=["t_mr"])
        b.op("vector", lambda e: e.reciprocal(
            out=t_mr[:, 1:2], in_=t_mr[:, 1:2]), reads=["t_mr"], writes=["t_mr"])
        b.op("vector", lambda e: e.tensor_scalar(
            out=t_xn[:], in0=xio[:], scalar1=t_mr.ap()[:, 0:1],
            scalar2=t_mr.ap()[:, 1:2], op0=ALU.subtract, op1=ALU.mult),
            reads=["xio", "t_mr"], writes=["t_xn"])
        for i in range(NDT):
            sl = i % 2
            b.op("tensor", lambda e, i=i, sl=sl: e.transpose(
                cx.pbt[sl].ap()[:, 0:128],
                t_xn[:, 128 * i:128 * (i + 1)], cx.id_bf[:]),
                reads=["t_xn", "id_bf"], writes=[f"pbt{sl}"])
            b.op("scalar", lambda e, i=i, tt=tt, sl=sl: e.copy(
                out=dst[:, i * dst_stride + 128 * tt:
                        i * dst_stride + 128 * (tt + 1)],
                in_=cx.pbt[sl].ap()[:, 0:128]),
                reads=[f"pbt{sl}"], writes=[f"{tag}{i}"])


def emit_attn0(b, cx, io, fused=False):
    nc = b.nc
    st = b._stack
    from contextlib import ExitStack
    NKT = [8, 16]
    stA = ExitStack()
    sbA = lambda n, s, d: stA.enter_context(nc.sbuf_tensor(n, s, d))
    if not fused:
        xln_kv = sbA("a_xlnkv", [128, NDT * 2048], BF16)
        xln_q = sbA("a_xlnq", [128, NDT * 1024], BF16)
    wcol = sbA("a_wcol", [128, 2 * 2048], F32)
    wcolb = sbA("a_wcolb", [128, 2 * 2048], BF16)
    t_ev = sbA("a_ev", [128, 2 * 512], BF16)
    if not fused:
        emit_tokln_transpose(b, cx, io["x_seq"], 16, xln_kv.ap(), 2048, "akv")
        emit_tokln_transpose(b, cx, io["x_own"], 8, xln_q.ap(), 1024, "aq")
    else:
        xln_q = cx.xln
        xln_kv = None
    for _ni, n in enumerate(list(range(16, 32)) + list(range(16))):
        if fused and _ni == 16:
            b.op("gpsimd", lambda e: e.collective_compute(
                "AllGather", ALU.bypass, replica_groups=io["pair_groups"],
                ins=[io["kT_own_flat"].opt()], outs=[io["kT_ag_flat"].opt()]),
                reads=["d_kT0", "d_kT1"], writes=["d_kT_ag"], cc_chan="cc_k0")
        sl = n % 2
        b.op("sync", lambda e, n=n, sl=sl: e.dma_start(
            out=wcol[:, 2048 * sl:2048 * (sl + 1)],
            in_=io["wqkv"][:, 128 * n:128 * (n + 1)].rearrange(
                "(k p) f -> p k f", p=128)),
            writes=[f"awc{sl}"], dma_chan=f"awc{sl}")
        b.op("vector", lambda e, sl=sl: e.tensor_copy(
            out=wcolb[:, 2048 * sl:2048 * (sl + 1)],
            in_=wcol[:, 2048 * sl:2048 * (sl + 1)]),
            reads=[f"awc{sl}"], writes=[f"awcb{sl}"])
        is_q = n < 16
        nck = 2 if (is_q or fused) else 4
        for c in range(nck):
            pbi = (2 * (n % 2) + c) % 4
            for k in range(16):
                if fused:
                    rhs_ap = cx.ln(k, c)
                    rtag = "xln"
                else:
                    src = xln_q if is_q else xln_kv
                    stride = 1024 if is_q else 2048
                    rtag = "aq" if is_q else "akv"
                    rhs_ap = src.ap()[:, k * stride + 512 * c:
                                      k * stride + 512 * (c + 1)]
                b.op("tensor", lambda e, sl=sl, k=k, rhs_ap=rhs_ap,
                     pbi=pbi: e.matmul(
                    cx.pb[pbi][:],
                    wcolb[:, 2048 * sl + 128 * k:2048 * sl + 128 * (k + 1)],
                    rhs_ap, start=(k == 0), stop=(k == 15)),
                    reads=[f"awcb{sl}", f"{rtag}{k}"], writes=[f"pb{pbi}"])
            b.op("scalar", lambda e, n=n, c=c, pbi=pbi: e.activation(
                out=t_ev[:, 512 * (pbi % 2):512 * (pbi % 2 + 1)],
                in_=cx.pb[pbi][:],
                func=AF.Identity, bias=io_bias_col(cx, n)),
                reads=[f"pb{pbi}", "bqkv"], writes=[f"aev{pbi%2}"])
            pe2 = pbi % 2
            if is_q:
                b.op("sync", lambda e, n=n, c=c, pe2=pe2: e.dma_start(
                    out=io["qT_d"][n, :, 512 * c:512 * (c + 1)],
                    in_=t_ev[:, 512 * pe2:512 * (pe2 + 1)]),
                    reads=[f"aev{pe2}"], writes=["d_qT"], dma_chan=f"aq{pe2}")
            elif fused:
                b.op("sync", lambda e, n=n, c=c, pe2=pe2: e.dma_start(
                    out=io["kT_own"][c, n - 16, :, :],
                    in_=t_ev[:, 512 * pe2:512 * (pe2 + 1)]),
                    reads=[f"aev{pe2}"], writes=[f"d_kT{pe2}"],
                    dma_chan=f"ak{pe2}")
            else:
                b.op("sync", lambda e, n=n, c=c, pe2=pe2: e.dma_start(
                    out=io["kT_d"][c, n - 16, :, :],
                    in_=t_ev[:, 512 * pe2:512 * (pe2 + 1)]),
                    reads=[f"aev{pe2}"], writes=["d_kT"], dma_chan=f"ak{pe2}")
    wvb = sbA("a_wvb", [128, 16 * 512], BF16)
    n_tt = 8 if fused else 16
    for fc in range(4):
        for k in range(16):
            sl = k % 2
            b.op("sync", lambda e, fc=fc, k=k, sl=sl: e.dma_start(
                out=wcol[:, 2048 * sl:2048 * sl + 512],
                in_=io["wqkv"][128 * k:128 * (k + 1),
                               4096 + 512 * fc:4096 + 512 * (fc + 1)]),
                writes=[f"awc{sl}"], dma_chan=f"awc{sl}")
            b.op("vector", lambda e, k=k, sl=sl: e.tensor_copy(
                out=wvb[:, 512 * k:512 * (k + 1)],
                in_=wcol[:, 2048 * sl:2048 * sl + 512]),
                reads=[f"awc{sl}"], writes=[f"awvb{k}"])
        for tt in range(n_tt):
            pbi = tt % 2
            for k in range(16):
                if fused:
                    lhs_ap = cx.xln.ap()[:, k * cx.T + 128 * tt:
                                         k * cx.T + 128 * (tt + 1)]
                    rtag = "xln"
                else:
                    lhs_ap = xln_kv[:, k * 2048 + 128 * tt:
                                    k * 2048 + 128 * (tt + 1)]
                    rtag = "akv"
                b.op("tensor", lambda e, k=k, lhs_ap=lhs_ap, pbi=pbi: e.matmul(
                    cx.pb[pbi][:], lhs_ap, wvb[:, 512 * k:512 * (k + 1)],
                    start=(k == 0), stop=(k == 15)),
                    reads=[f"awvb{k}", f"{rtag}{k}"], writes=[f"pb{pbi}"])
            b.op("scalar", lambda e, pbi=pbi: e.copy(
                out=t_ev[:, 512 * pbi:512 * (pbi + 1)], in_=cx.pb[pbi][:]),
                reads=[f"pb{pbi}"], writes=[f"aev{pbi}"])
            if fused:
                b.op("sync", lambda e, tt=tt, fc=fc, pbi=pbi: e.dma_start(
                    out=io["v_own"][tt // 4, tt % 4, :, 512 * fc:512 * (fc + 1)],
                    in_=t_ev[:, 512 * pbi:512 * (pbi + 1)]),
                    reads=[f"aev{pbi}"], writes=[f"d_v{pbi}"],
                    dma_chan=f"av{pbi}")
            else:
                b.op("sync", lambda e, tt=tt, fc=fc, pbi=pbi: e.dma_start(
                    out=io["v_d"][tt, :, 512 * fc:512 * (fc + 1)],
                    in_=t_ev[:, 512 * pbi:512 * (pbi + 1)]),
                    reads=[f"aev{pbi}"], writes=["d_v"], dma_chan=f"av{pbi}")
    if fused:
        b.op("gpsimd", lambda e: e.collective_compute(
            "AllGather", ALU.bypass, replica_groups=io["pair_groups"],
            ins=[io["v_own_flat"].opt()], outs=[io["v_ag_flat"].opt()]),
            reads=["d_v0", "d_v1"], writes=["d_v_ag"], cc_chan="cc_v0")
    stA.close()
    b.fence()
    stAO = ExitStack()
    ao = stAO.enter_context(nc.sbuf_tensor("b_ao", [128, 16 * 1024], BF16))
    stB = ExitStack()
    sbB = lambda n, s, d: stB.enter_context(nc.sbuf_tensor(n, s, d))
    q_sb = sbB("b_q", [128, 2 * 1024], BF16)
    k_sb = sbB("b_k", [128, 2 * 2048], BF16)
    v_sb = sbB("b_v", [128, 2 * 2048], BF16)
    ebuf = sbB("b_e", [128, 4 * 512], BF16)
    amask = sbB("b_mask", [128, 16 * 512], BF16)
    t_mh = sbB("b_mh", [128, 2], F32)
    b.op("gpsimd", lambda e: e.dma_start(
        out=amask.ap().rearrange("p (k f) -> p k f", f=512),
        in_=io["cmask16"].transpose([1, 0, 2])),
        writes=["amask"], dma_chan="amask")
    R = cx.rows.ap()
    r_qn = R[:, 0:1024]
    r_kn = R[:, 1024:3072]
    r_sc = R[:, 3072:3084]
    def headA(h):
        hs = h % 2
        b.op("sync", lambda e, h=h, hs=hs: e.dma_start(
            out=q_sb[:, 1024 * hs:1024 * (hs + 1)], in_=io["qT_d"][h]),
            reads=["d_qT"], writes=[f"bq{hs}"], dma_chan=f"bq{hs}")
        if fused:
            kagv = io["kT_ag"]
            vagv = io["v_ag"]
            for rk in range(2):
                for s_ in range(2):
                    qg = io["gmap"][(rk, s_)]
                    b.op("sync", lambda e, h=h, hs=hs, rk=rk, s_=s_, qg=qg:
                         e.dma_start(
                        out=k_sb[:, 2048 * hs + 512 * qg:
                                 2048 * hs + 512 * (qg + 1)],
                        in_=kagv[rk, s_, h]),
                        reads=["d_kT_ag"], writes=[f"bk{hs}"],
                        dma_chan=f"bk{hs}")
                    b.op("sync", lambda e, h=h, hs=hs, rk=rk, s_=s_, qg=qg:
                         e.dma_start(
                        out=v_sb.ap()[:, 2048 * hs + 512 * qg:
                                      2048 * hs + 512 * (qg + 1)].rearrange(
                            "p (t f) -> p t f", f=128),
                        in_=vagv[rk, s_, :, :,
                                 128 * h:128 * (h + 1)].transpose([1, 0, 2])),
                        reads=["d_v_ag"], writes=[f"bv{hs}"],
                        dma_chan=f"bv{hs}")
        else:
            b.op("sync", lambda e, h=h, hs=hs: e.dma_start(
                out=k_sb.ap().rearrange("p (sl g t) -> p sl g t", sl=2, t=512
                                        )[:, hs],
                in_=io["kT_d"][:, h, :, :].transpose([1, 0, 2])),
                reads=["d_kT"], writes=[f"bk{hs}"], dma_chan=f"bk{hs}")
            b.op("sync", lambda e, h=h, hs=hs: e.dma_start(
                out=v_sb.ap().rearrange("p (sl g t) -> p sl g t", sl=2, t=128
                                        )[:, hs],
                in_=io["v_d"][:, :, 128 * h:128 * (h + 1)].transpose([1, 0, 2])),
                reads=["d_v"], writes=[f"bv{hs}"], dma_chan=f"bv{hs}")
        for part, (buf, reg, row, nch) in enumerate(
                [(q_sb, f"bq{hs}", r_qn, 2), (k_sb, f"bk{hs}", r_kn, 4)]):
            for c in range(nch):
                sq = cx.t_sq.ap()[:, 512 * (c % 2):512 * (c % 2 + 1)]
                off = 1024 * hs if nch == 2 else 2048 * hs
                b.op("scalar", lambda e, c=c, buf=buf, sq=sq, off=off:
                     e.activation(
                    out=sq, in_=buf[:, off + 512 * c:off + 512 * (c + 1)],
                    func=AF.Square),
                    reads=[reg], writes=[f"t_sq{c%2}"])
                b.op("tensor", lambda e, c=c, sq=sq: e.matmul(
                    cx.pb[4].ap()[0:1, :], cx.ones_bf[:], sq,
                    start=True, stop=True),
                    reads=[f"t_sq{c%2}", "ones_bf"], writes=["pb4"])
                b.op("scalar", lambda e, c=c, row=row: e.copy(
                    out=row[:, 512 * c:512 * (c + 1)], in_=cx.pb[4].ap()[0:1, :]),
                    reads=["pb4"], writes=["r_qn"])
        b.op("vector", lambda e: e.reduce_max(
            out=r_sc[:, 0:1], in_=r_qn, axis=AX.X), reads=["r_qn"],
            writes=["r_sc"])
        b.op("vector", lambda e: e.reduce_max(
            out=r_sc[:, 1:2], in_=r_kn, axis=AX.X), reads=["r_qn", "r_sc"],
            writes=["r_sc"])
        b.op("vector", lambda e: e.tensor_tensor(
            out=r_sc[:, 2:3], in0=r_sc[:, 0:1], in1=r_sc[:, 1:2], op=ALU.mult),
            reads=["r_sc"], writes=["r_sc"])
        b.op("scalar", lambda e: e.activation(
            out=r_sc[:, 2:3], in_=r_sc[:, 2:3], func=AF.Sqrt),
            reads=["r_sc", "cpool"], writes=["r_sc"])
        b.op("vector", lambda e: e.tensor_scalar(
            out=r_sc[:, 2:3], in0=r_sc[:, 2:3],
            scalar1=-1.0 / math.sqrt(128.0), scalar2=None, op0=ALU.mult),
            reads=["r_sc"], writes=["r_sc"])
        b.op("tensor", lambda e: e.matmul(
            cx.pb[4].ap()[:, 0:1], cx.onesrow[:], r_sc[:, 2:3],
            start=True, stop=True), reads=["r_sc", "onesrow"], writes=["pb4"])
        b.op("scalar", lambda e, hs=hs: e.copy(
            out=t_mh[:, hs:hs + 1], in_=cx.pb[4].ap()[:, 0:1]),
            reads=["pb4"], writes=[f"t_mh{hs}"])

    av_sb = cx.xio
    res_sb = cx.xio.ap()[0:1, 1024:2048]

    def norm_finish(pend):
        h, s, par = pend
        r_es = res_sb[0:1, 512 * par:512 * (par + 1)]
        b.op("tensor", lambda e, r_es=r_es: e.matmul(
            cx.pb[3][:], cx.onesrow[:], r_es, start=True, stop=True),
            reads=[f"r_es{par}", "onesrow"], writes=["pb3"])
        b.op("scalar", lambda e: e.copy(out=cx.t_ab[:], in_=cx.pb[3][:]),
             reads=["pb3"], writes=["t_ab"])
        b.op("vector", lambda e, par=par: e.tensor_tensor(
            out=cx.t_tmp[:, 0:512], in0=av_sb[:, 512 * par:512 * (par + 1)],
            in1=cx.t_ab[:], op=ALU.mult),
            reads=[f"b_avs{par}", "t_ab"], writes=["t_tmp0"])
        b.op("scalar", lambda e, h=h, s=s: e.activation(
            out=ao[:, 1024 * h + 512 * s:1024 * h + 512 * (s + 1)],
            in_=cx.t_tmp[:, 0:512], func=AF.Identity,
            bias=io_bias_col(cx, 32 + h)),
            reads=["t_tmp0", "bqkv"], writes=[f"ao{h}"])

    pending = []

    def headB(h):
        hs = h % 2
        for s in range(2):
            if pending:
                norm_finish(pending.pop(0))
            n_kt = NKT[s]
            SCAP = [cx.pb[0].ap(), cx.pb[1].ap(), cx.pb[5].ap(),
                    cx.pbt[0].ap().bitcast(F32), cx.pbt[1].ap().bitcast(F32)]
            SCRG = ["pb0", "pb1", "pb5", "pbt0", "pbt1"]

            def sc_stage(kt, s=s, hs=hs):
                pbi = kt % 5
                b.op("tensor", lambda e, kt=kt, pbi=pbi, hs=hs, s=s: e.matmul(
                    SCAP[pbi][:, 0:512],
                    k_sb[:, 2048 * hs + 128 * kt:2048 * hs + 128 * (kt + 1)],
                    q_sb[:, 1024 * hs + 512 * s:1024 * hs + 512 * (s + 1)],
                    start=True, stop=True),
                    reads=[f"bk{hs}", f"bq{hs}"], writes=[SCRG[pbi]])

            def ev_stage(kt, s=s, hs=hs):
                pbi = kt % 5
                eo = 512 * (kt % 4)
                er = f"eb{kt % 4}"
                b.op("scalar", lambda e, eo=eo, pbi=pbi, hs=hs: e.activation(
                    out=ebuf[:, eo:eo + 512], in_=SCAP[pbi][:, 0:512],
                    func=AF.Exp, bias=t_mh.ap()[:, hs:hs + 1],
                    scale=cx.cv(1.0 / math.sqrt(128.0))),
                    reads=[SCRG[pbi], f"t_mh{hs}", "cpool"], writes=[er])
                masked = (kt < 8) if s == 0 else (kt >= 8)
                if masked:
                    moff = kt * 512
                    b.op("vector", lambda e, eo=eo, moff=moff: e.tensor_tensor(
                        out=ebuf[:, eo:eo + 512],
                        in0=ebuf[:, eo:eo + 512],
                        in1=amask[:, moff:moff + 512], op=ALU.mult),
                        reads=[er, "amask"], writes=[er])
                b.op("tensor", lambda e, eo=eo, kt=kt, n=n_kt: e.matmul(
                    cx.pb[3].ap()[0:1, :], cx.ones_bf[:],
                    ebuf[:, eo:eo + 512],
                    start=(kt == 0), stop=(kt == n - 1)),
                    reads=[er, "ones_bf"], writes=["pb3"])
                b.op("tensor", lambda e, eo=eo, kt=kt, n=n_kt, hs=hs: e.matmul(
                    cx.pb[2][:],
                    v_sb[:, 2048 * hs + 128 * kt:2048 * hs + 128 * (kt + 1)],
                    ebuf[:, eo:eo + 512],
                    start=(kt == 0), stop=(kt == n - 1)),
                    reads=[f"bv{hs}", er], writes=["pb2"])

            for kt in range(n_kt + 4):
                if kt < n_kt:
                    sc_stage(kt)
                if kt >= 4:
                    ev_stage(kt - 4)
            par = (2 * h + s) % 2
            r_es = res_sb[0:1, 512 * par:512 * (par + 1)]
            b.op("scalar", lambda e, par=par: e.copy(
                out=av_sb[:, 512 * par:512 * (par + 1)], in_=cx.pb[2][:]),
                reads=["pb2"], writes=[f"b_avs{par}"])
            b.op("scalar", lambda e, r_es=r_es, par=par: e.copy(
                out=r_es, in_=cx.pb[3].ap()[0:1, :]),
                reads=["pb3"], writes=[f"r_es{par}"])
            b.op("scalar", lambda e, r_es=r_es, par=par: e.activation(
                out=r_es, in_=r_es, func=AF.Ln),
                reads=[f"r_es{par}", "cpool"], writes=[f"r_es{par}"])
            b.op("scalar", lambda e, r_es=r_es, par=par: e.activation(
                out=r_es, in_=r_es, func=AF.Exp, scale=cx.cv1(-1.0)),
                reads=[f"r_es{par}", "cpool"], writes=[f"r_es{par}"])
            pending.append((h, s, par))

    headA(0)
    for h in range(16):
        if h + 1 < 16:
            headA(h + 1)
        headB(h)
    while pending:
        norm_finish(pending.pop(0))
    stB.close()
    b.fence()
    stC = ExitStack()
    sbC = lambda n, s, d: stC.enter_context(nc.sbuf_tensor(n, s, d))
    if not fused:
        cx.xres = sbC("c_xres", [128, NDT * 1024], F32)
    wcol2 = sbC("c_wcol", [128, 2 * 2048], F32)
    wcol2b = sbC("c_wcolb", [128, 2 * 2048], BF16)
    if not fused:
        emit_loadx(b, cx, io["x_own"])
    for n in range(16):
        sl = n % 2
        b.op("sync", lambda e, n=n, sl=sl: e.dma_start(
            out=wcol2[:, 2048 * sl:2048 * (sl + 1)],
            in_=io["wo"][:, 128 * n:128 * (n + 1)].rearrange(
                "(k p) f -> p k f", p=128)),
            writes=[f"cwc{sl}"], dma_chan=f"cwc{sl}")
        b.op("vector", lambda e, sl=sl: e.tensor_copy(
            out=wcol2b[:, 2048 * sl:2048 * (sl + 1)],
            in_=wcol2[:, 2048 * sl:2048 * (sl + 1)]),
            reads=[f"cwc{sl}"], writes=[f"cwcb{sl}"])
        for c in range(2):
            pbi = c % 2
            for k in range(16):
                b.op("tensor", lambda e, n=n, sl=sl, c=c, k=k, pbi=pbi: e.matmul(
                    cx.pb[pbi][:],
                    wcol2b[:, 2048 * sl + 128 * k:2048 * sl + 128 * (k + 1)],
                    ao[:, 1024 * k + 512 * c:1024 * k + 512 * (c + 1)],
                    start=(k == 0), stop=(k == 15)),
                    reads=[f"cwcb{sl}", f"ao{k}"], writes=[f"pb{pbi}"])
            b.op("scalar", lambda e, n=n, pbi=pbi: e.activation(
                out=cx.t_tmp[:, 512 * pbi:512 * (pbi + 1)], in_=cx.pb[pbi][:],
                func=AF.Identity, bias=io["bo_cols_ap"][:, n:n + 1]),
                reads=[f"pb{pbi}", "bo"], writes=[f"t_tmp{pbi}"])
            b.op("gpsimd", lambda e, n=n, c=c, pbi=pbi: e.tensor_tensor(
                out=cx.x(n, c), in0=cx.x(n, c),
                in1=cx.t_tmp[:, 512 * pbi:512 * (pbi + 1)], op=ALU.add),
                reads=[f"t_tmp{pbi}", f"x{n}"], writes=[f"x{n}"])
    toks = emit_storex(b, cx, io["x_out"]) if not fused else []
    stC.close()
    stAO.close()
    b.fence()
    return toks


def emit_ffn1(b, cx, io, tag="f1"):
    nc = b.nc
    from contextlib import ExitStack
    stF = ExitStack()
    sbF = lambda n, s, d: stF.enter_context(nc.sbuf_tensor(n, s, d))
    wbuf = sbF(f"{tag}_wbuf", [128, 4 * 2048], BF16)
    wdbuf = sbF(f"{tag}_wdbuf", [128, 2 * 16 * 256], BF16)
    t_g = sbF(f"{tag}_g", [128, 512], BF16)
    t_u = sbF(f"{tag}_u", [128, 512], BF16)
    h = sbF(f"{tag}_h", [128, 16 * 512], BF16)
    wug_tile, wdn1_tile = io["wug_tile"], io["wdn1_tile"]
    wrd = io.get("w_regions", [])
    wsu, wsg, wsd = io["wsu_sb"], io["wsg_sb"], io["wsd_sb"]

    for tc in range(cx.NC):
        for oc in range(4):
            for j in range(16):
                m_u = 16 * oc + j
                m_g = 64 + m_u
                sl_u, sl_g = 2 * (j % 2), 2 * (j % 2) + 1
                for qi, (m, sl) in enumerate(((m_u, sl_u), (m_g, sl_g))):
                    b.op("sync" if qi == 0 else "scalar",
                         lambda e, m=m, sl=sl: e.dma_start(
                        out=wbuf.ap().rearrange("p (s k f) -> p s k f", s=4,
                                                f=128)[:, sl, :, :],
                        in_=wug_tile(m)),
                        reads=wrd, writes=[f"f1w{sl}"], dma_chan=f"f1w{sl}")
                pg, pu = (j % 2), 2 + (j % 2)
                for k in range(16):
                    b.op("tensor", lambda e, k=k, sl=sl_u, tc=tc,
                         pu=pu: e.matmul(
                        cx.pb[pu][:],
                        wbuf[:, 2048 * sl + 128 * k:2048 * sl + 128 * (k + 1)],
                        cx.ln(k, tc), start=(k == 0), stop=(k == 15)),
                        reads=[f"f1w{sl_u}", f"xln{k}"], writes=[f"pb{pu}"])
                for k in range(16):
                    b.op("tensor", lambda e, k=k, sl=sl_g, tc=tc,
                         pg=pg: e.matmul(
                        cx.pb[pg][:],
                        wbuf[:, 2048 * sl + 128 * k:2048 * sl + 128 * (k + 1)],
                        cx.ln(k, tc), start=(k == 0), stop=(k == 15)),
                        reads=[f"f1w{sl_g}", f"xln{k}"], writes=[f"pb{pg}"])
                b.op("scalar", lambda e, m=m_u, pu=pu: e.activation(
                    out=t_u[:], in_=cx.pb[pu][:], func=AF.Identity,
                    scale=wsu[:, m:m + 1]),
                    reads=[f"pb{pu}", "wsu"], writes=["f1_u"])
                b.op("scalar", lambda e, m=m_u, pg=pg: e.activation(
                    out=t_g[:], in_=cx.pb[pg][:], func=AF.Silu,
                    scale=wsg[:, m:m + 1]),
                    reads=[f"pb{pg}", "wsg"], writes=["f1_g"])
                b.op("vector", lambda e, j=j: e.tensor_tensor(
                    out=h[:, 512 * j:512 * (j + 1)],
                    in0=t_g[:], in1=t_u[:], op=ALU.mult),
                    reads=["f1_g", "f1_u"], writes=[f"f1h{j}"])
            for j2 in range(8):
                sl = j2 % 2
                b.op("sync", lambda e, oc=oc, j2=j2, sl=sl: e.dma_start(
                    out=wdbuf.ap().rearrange("p (s k f) -> p s k f", s=2,
                                             f=256)[:, sl, :, :],
                    in_=wdn1_tile(oc, j2)),
                    reads=wrd, writes=[f"f1d{sl}"], dma_chan=f"f1d{sl}")
                for dt in range(2):
                    i_out = 2 * j2 + dt
                    pbi = 4 + dt
                    for k in range(16):
                        b.op("tensor", lambda e, k=k, sl=sl, dt=dt,
                             pbi=pbi: e.matmul(
                            cx.pb[pbi][:],
                            wdbuf[:, 4096 * sl + 256 * k + 128 * dt:
                                  4096 * sl + 256 * k + 128 * (dt + 1)],
                            h[:, 512 * k:512 * (k + 1)],
                            start=(k == 0), stop=(k == 15)),
                            reads=[f"f1d{sl}", f"f1h{k}"], writes=[f"pb{pbi}"])
                    b.op("vector", lambda e, i=i_out, tc=tc, pbi=pbi:
                         e.scalar_tensor_tensor(
                        out=cx.x(i, tc), in0=cx.pb[pbi][:],
                        scalar=wsd[:, i:i + 1], in1=cx.x(i, tc),
                        op0=ALU.mult, op1=ALU.add),
                        reads=[f"pb{pbi}", "wsd", f"x{i_out}"],
                        writes=[f"x{i_out}"])
    stF.close()


def emit_rope(b, cx, src_reg, src_ap, cos_ap, sin_ap, cos_reg, out_cb, npart):
    b.op("tensor", lambda e: e.matmul(
        cx.pb[5].ap()[0:npart, :], cx.ropeRT.ap()[0:npart, 0:npart], src_ap,
        start=True, stop=True),
        reads=[src_reg, "ropeRT"], writes=["pb5"])
    b.op("vector", lambda e: e.tensor_tensor(
        out=cx.t_tmp[:, 0:512][0:npart, :], in0=src_ap, in1=cos_ap,
        op=ALU.mult), reads=[src_reg, cos_reg], writes=["t_tmp0"])
    b.op("vector", lambda e: e.tensor_tensor(
        out=cx.t_tmp[:, 512:1024][0:npart, :], in0=cx.pb[5].ap()[0:npart, :],
        in1=sin_ap, op=ALU.mult),
        reads=["pb5", cos_reg.replace("cos", "sin")], writes=["t_tmp1"])
    b.op("vector", lambda e: e.tensor_tensor(
        out=cx.t_tmp[:, 0:512][0:npart, :], in0=cx.t_tmp[:, 0:512][0:npart, :],
        in1=cx.t_tmp[:, 512:1024][0:npart, :], op=ALU.add),
        reads=["t_tmp0", "t_tmp1"], writes=["t_tmp0"])
    out_cb()


def emit_attn1(b, cx, io, fused=False):
    nc = b.nc
    st = b._stack
    from contextlib import ExitStack
    NKT = [8, 16]
    stA = ExitStack()
    sbA = lambda n, s, d: stA.enter_context(nc.sbuf_tensor(n, s, d))
    if not fused:
        xln_kv = sbA("a1_xlnkv", [128, NDT * 2048], BF16)
        xln_q = sbA("a1_xlnq", [128, NDT * 1024], BF16)
    else:
        xln_kv = None
        xln_q = None
    wcol = sbA("a1_wcol", [128, 2 * 2048], F32)
    wcolb = sbA("a1_wcolb", [128, 2 * 2048], BF16)
    t_ev = sbA("a1_ev", [128, 2 * 512], BF16)
    t_ro = sbA("a1_ro", [128, 512], BF16)
    stR = ExitStack()
    sbR = lambda n, s, d: stR.enter_context(nc.sbuf_tensor(n, s, d))
    cos_own = sbR("a1_co", [128, 1024], F32)
    sin_own = sbR("a1_so", [128, 1024], F32)
    if not fused:
        cos_seq = sbR("a1_cs", [128, 2048], F32)
        sin_seq = sbR("a1_ss", [128, 2048], F32)
    else:
        cos_seq = sin_seq = None
    tabs = [("cos_own", cos_own), ("sin_own", sin_own)]
    if not fused:
        tabs += [("cos_seq", cos_seq), ("sin_seq", sin_seq)]
    for nm, t in tabs:
        b.op("gpsimd", lambda e, t=t, nm=nm: e.dma_start(
            out=t[:], in_=io[nm]), writes=[nm], dma_chan=f"c_{nm}")
    if not fused:
        emit_tokln_transpose(b, cx, io["x_seq"], 16, xln_kv.ap(), 2048, "akv")
        emit_tokln_transpose(b, cx, io["x_own"], 8, xln_q.ap(), 1024, "aq")
    for _ni, n in enumerate(list(range(16, 20)) + list(range(16))):
        if fused and _ni == 4:
            b.op("gpsimd", lambda e: e.collective_compute(
                "AllGather", ALU.bypass, replica_groups=io["pair_groups"],
                ins=[io["kT1_own_flat"].opt()], outs=[io["kT1_ag_flat"].opt()]),
                reads=["d_kT0"], writes=["d_kT_ag"], cc_chan="cc_k1")
        is_q = n < 16
        sl = n % 2
        if is_q:
            wsrc = io["wq"]
            coff = 128 * n
        else:
            wsrc = io["wkv"]
            coff = 128 * (n - 16)
        b.op("sync", lambda e, wsrc=wsrc, coff=coff, sl=sl: e.dma_start(
            out=wcol[:, 2048 * sl:2048 * (sl + 1)],
            in_=wsrc[:, coff:coff + 128].rearrange("(k p) f -> p k f", p=128)),
            writes=[f"awc{sl}"], dma_chan=f"awc{sl}")
        b.op("vector", lambda e, sl=sl: e.tensor_copy(
            out=wcolb[:, 2048 * sl:2048 * (sl + 1)],
            in_=wcol[:, 2048 * sl:2048 * (sl + 1)]),
            reads=[f"awc{sl}"], writes=[f"awcb{sl}"])
        nck = 2 if (is_q or fused) else 4
        for c in range(nck):
            pbi = c % 2
            for k in range(16):
                if fused:
                    rhs_ap = cx.ln(k, c)
                    rtag = "xln"
                else:
                    src = xln_q if is_q else xln_kv
                    stride = 1024 if is_q else 2048
                    rtag = "aq" if is_q else "akv"
                    rhs_ap = src.ap()[:, k * stride + 512 * c:
                                      k * stride + 512 * (c + 1)]
                b.op("tensor", lambda e, sl=sl, k=k, rhs_ap=rhs_ap,
                     pbi=pbi: e.matmul(
                    cx.pb[pbi][:],
                    wcolb[:, 2048 * sl + 128 * k:2048 * sl + 128 * (k + 1)],
                    rhs_ap, start=(k == 0), stop=(k == 15)),
                    reads=[f"awcb{sl}", f"{rtag}{k}"], writes=[f"pb{pbi}"])
            b.op("scalar", lambda e, pbi=pbi: e.copy(
                out=t_ro[:], in_=cx.pb[pbi][:]),
                reads=[f"pb{pbi}"], writes=["a1ro"])
            ct = cos_own if (is_q or fused) else cos_seq
            stt = sin_own if (is_q or fused) else sin_seq
            creg = "cos_own" if (is_q or fused) else "cos_seq"
            emit_rope(b, cx, "a1ro", t_ro.ap(), ct.ap()[:, 512 * c:512 * (c + 1)],
                      stt.ap()[:, 512 * c:512 * (c + 1)], creg, lambda: None, 128)
            pbe = pbi
            b.op("vector", lambda e: e.tensor_copy(
                out=t_ev[:, 0:512], in_=cx.t_tmp[:, 0:512]),
                reads=["t_tmp0"], writes=["aev0"])
            if is_q:
                b.op("sync", lambda e, n=n, c=c: e.dma_start(
                    out=io["qT_d"][n, :, 512 * c:512 * (c + 1)],
                    in_=t_ev[:, 0:512]),
                    reads=["aev0"], writes=["d_qT"], dma_chan="aq0")
            elif fused:
                b.op("sync", lambda e, n=n, c=c: e.dma_start(
                    out=io["kT1_own"][c, n - 16, :, :], in_=t_ev[:, 0:512]),
                    reads=["aev0"], writes=["d_kT0"], dma_chan="ak0")
            else:
                b.op("sync", lambda e, n=n, c=c: e.dma_start(
                    out=io["kT_d"][c, n - 16, :, :], in_=t_ev[:, 0:512]),
                    reads=["aev0"], writes=["d_kT"], dma_chan="ak0")
    stR.close()
    b.fence()
    wvb = sbA("a1_wvb", [128, 16 * 512], BF16)
    for k in range(16):
        sl = k % 2
        b.op("sync", lambda e, k=k, sl=sl: e.dma_start(
            out=wcol[:, 2048 * sl:2048 * sl + 512],
            in_=io["wkv"][128 * k:128 * (k + 1), 512:1024]),
            writes=[f"awc{sl}"], dma_chan=f"awc{sl}")
        b.op("vector", lambda e, k=k, sl=sl: e.tensor_copy(
            out=wvb[:, 512 * k:512 * (k + 1)],
            in_=wcol[:, 2048 * sl:2048 * sl + 512]),
            reads=[f"awc{sl}"], writes=[f"awvb{k}"])
    for tt in range(8 if fused else 16):
        pbi = tt % 2
        for k in range(16):
            if fused:
                lhs_ap = cx.xln.ap()[:, k * cx.T + 128 * tt:
                                     k * cx.T + 128 * (tt + 1)]
                rtag = "xln"
            else:
                lhs_ap = xln_kv[:, k * 2048 + 128 * tt:
                                k * 2048 + 128 * (tt + 1)]
                rtag = "akv"
            b.op("tensor", lambda e, k=k, lhs_ap=lhs_ap, pbi=pbi: e.matmul(
                cx.pb[pbi][:], lhs_ap, wvb[:, 512 * k:512 * (k + 1)],
                start=(k == 0), stop=(k == 15)),
                reads=[f"awvb{k}", f"{rtag}{k}"], writes=[f"pb{pbi}"])
        b.op("scalar", lambda e, pbi=pbi: e.copy(
            out=t_ev[:, 512 * pbi:512 * (pbi + 1)], in_=cx.pb[pbi][:]),
            reads=[f"pb{pbi}"], writes=[f"aev{pbi}"])
        if fused:
            b.op("sync", lambda e, tt=tt, pbi=pbi: e.dma_start(
                out=io["v1_own"][tt // 4, tt % 4],
                in_=t_ev[:, 512 * pbi:512 * (pbi + 1)]),
                reads=[f"aev{pbi}"], writes=[f"d_v{pbi}"], dma_chan=f"av{pbi}")
        else:
            b.op("sync", lambda e, tt=tt, pbi=pbi: e.dma_start(
                out=io["v_d1"][tt], in_=t_ev[:, 512 * pbi:512 * (pbi + 1)]),
                reads=[f"aev{pbi}"], writes=["d_v"], dma_chan=f"av{pbi}")
    if fused:
        b.op("gpsimd", lambda e: e.collective_compute(
            "AllGather", ALU.bypass, replica_groups=io["pair_groups"],
            ins=[io["v1_own_flat"].opt()], outs=[io["v1_ag_flat"].opt()]),
            reads=["d_v0", "d_v1"], writes=["d_v_ag"], cc_chan="cc_v1")
    stA.close()
    b.fence()
    stAO = ExitStack()
    ao = stAO.enter_context(nc.sbuf_tensor("b1_ao", [128, 16 * 1024], BF16))
    stB = ExitStack()
    sbB = lambda n, s, d: stB.enter_context(nc.sbuf_tensor(n, s, d))
    q_sb = sbB("b1_q", [128, 2 * 1024], BF16)
    k_sb = sbB("b1_k", [128, 2 * 2048], BF16)
    v_sb = sbB("b1_v", [128, 2 * 16 * 65], BF16)
    ebuf = sbB("b1_e", [128, 4 * 512], BF16)
    amask = sbB("b1_mask", [128, 16 * 512], BF16)
    t_mh = sbB("b1_mh", [128, 2], F32)
    b.op("gpsimd", lambda e: e.dma_start(
        out=amask.ap().rearrange("p (k f) -> p k f", f=512),
        in_=io["cmask16"].transpose([1, 0, 2])),
        writes=["amask"], dma_chan="amask")
    R = cx.rows.ap()
    r_qn = R[:, 0:1024]
    r_kn = R[:, 1024:3072]
    r_sc = R[:, 3072:3084]
    def groupA(g):
        gs = g % 2
        ko = 2048 * gs
        vo = 1040 * gs
        if fused:
            for rk in range(2):
                for s_ in range(2):
                    qg = io["gmap"][(rk, s_)]
                    b.op("sync", lambda e, g=g, gs=gs, rk=rk, s_=s_, qg=qg,
                         ko=ko: e.dma_start(
                        out=k_sb[0:64, ko + 512 * qg:ko + 512 * (qg + 1)],
                        in_=io["kT1_ag"][rk, s_, g // 2,
                                         64 * (g % 2):64 * (g % 2) + 64, :]),
                        reads=["d_kT_ag"], writes=[f"b1k{gs}"],
                        dma_chan=f"b1k{gs}")
        else:
            b.op("sync", lambda e, g=g, gs=gs, ko=ko: e.dma_start(
                out=k_sb.ap()[:, ko:ko + 2048].rearrange(
                    "p (g2 t) -> p g2 t", t=512)[0:64],
                in_=io["kT_d"][:, g // 2, 64 * (g % 2):64 * (g % 2) + 64,
                               :].transpose([1, 0, 2])),
                reads=["d_kT"], writes=[f"b1k{gs}"], dma_chan=f"b1k{gs}")
        b.op("vector", lambda e, vo=vo, gs=gs: e.memset(
            v_sb.ap()[:, vo:vo + 1040].rearrange(
                "p (k f) -> p k f", f=65)[:, :, 64:65], 1.0),
            writes=[f"b1v{gs}"], reads=[f"b1v{gs}"])
        if fused:
            for rk in range(2):
                for s_ in range(2):
                    qg = io["gmap"][(rk, s_)]
                    b.op("sync", lambda e, g=g, gs=gs, rk=rk, s_=s_, qg=qg,
                         vo=vo: e.dma_start(
                        out=v_sb.ap()[:, vo + 260 * qg:
                                      vo + 260 * (qg + 1)].rearrange(
                            "p (t f) -> p t f", f=65)[:, :, 0:64],
                        in_=io["v1_ag"][rk, s_, :, :,
                                        64 * g:64 * (g + 1)].transpose(
                            [1, 0, 2])),
                        reads=["d_v_ag", f"b1v{gs}"], writes=[f"b1v{gs}"],
                        dma_chan=f"b1v{gs}")
        else:
            b.op("sync", lambda e, g=g, gs=gs, vo=vo: e.dma_start(
                out=v_sb.ap()[:, vo:vo + 1040].rearrange(
                    "p (k f) -> p k f", f=65)[:, :, 0:64],
                in_=io["v_d1"][:, :, 64 * g:64 * (g + 1)].transpose([1, 0, 2])),
                reads=["d_v", f"b1v{gs}"], writes=[f"b1v{gs}"],
                dma_chan=f"b1v{gs}")
        for c in range(4):
            sq = cx.t_sq.ap()[0:64, 512 * (c % 2):512 * (c % 2 + 1)]
            b.op("scalar", lambda e, c=c, sq=sq, ko=ko, gs=gs: e.activation(
                out=sq, in_=k_sb[0:64, ko + 512 * c:ko + 512 * (c + 1)],
                func=AF.Square),
                reads=[f"b1k{gs}"], writes=[f"t_sq{c%2}"])
            b.op("tensor", lambda e, c=c, sq=sq: e.matmul(
                cx.pb[4].ap()[0:1, :], cx.ones_bf[0:64, :], sq,
                start=True, stop=True),
                reads=[f"t_sq{c%2}", "ones_bf"], writes=["pb4"])
            b.op("scalar", lambda e, c=c, gs=gs: e.copy(
                out=r_kn[:, 2048 * 0 + 512 * c:512 * (c + 1)]
                if gs == 0 else r_kn[:, 512 * c:512 * (c + 1)],
                in_=cx.pb[4].ap()[0:1, :]),
                reads=["pb4"], writes=["r_qn"])
        b.op("vector", lambda e, gs=gs: e.reduce_max(
            out=r_sc[:, 4 + gs:5 + gs], in_=r_kn, axis=AX.X),
            reads=["r_qn"], writes=["r_sc"])

    def hqA(h):
        hs = h % 2
        g = h // 4
        gs = (g % 2)
        b.op("sync", lambda e, h=h, hs=hs: e.dma_start(
            out=q_sb[0:64, 1024 * hs:1024 * (hs + 1)],
            in_=io["qT_d"][h // 2, 64 * (h % 2):64 * (h % 2) + 64, :]),
            reads=["d_qT"], writes=[f"b1q{hs}"], dma_chan=f"b1q{hs}")
        for c in range(2):
            sq = cx.t_sq.ap()[0:64, 512 * (c % 2):512 * (c % 2 + 1)]
            b.op("scalar", lambda e, c=c, hs=hs, sq=sq: e.activation(
                out=sq, in_=q_sb[0:64, 1024 * hs + 512 * c:
                                 1024 * hs + 512 * (c + 1)],
                func=AF.Square), reads=[f"b1q{hs}"], writes=[f"t_sq{c%2}"])
            b.op("tensor", lambda e, c=c, sq=sq: e.matmul(
                cx.pb[4].ap()[0:1, :], cx.ones_bf[0:64, :], sq,
                start=True, stop=True),
                reads=[f"t_sq{c%2}", "ones_bf"], writes=["pb4"])
            b.op("scalar", lambda e, c=c: e.copy(
                out=r_qn[:, 512 * c:512 * (c + 1)],
                in_=cx.pb[4].ap()[0:1, :]), reads=["pb4"], writes=["r_qn"])
        b.op("vector", lambda e: e.reduce_max(
            out=r_sc[:, 0:1], in_=r_qn[:, 0:1024], axis=AX.X),
            reads=["r_qn", "r_sc"], writes=["r_sc"])
        b.op("vector", lambda e, gs=gs: e.tensor_tensor(
            out=r_sc[:, 2:3], in0=r_sc[:, 0:1], in1=r_sc[:, 4 + gs:5 + gs],
            op=ALU.mult), reads=["r_sc"], writes=["r_sc"])
        b.op("scalar", lambda e: e.activation(
            out=r_sc[:, 2:3], in_=r_sc[:, 2:3], func=AF.Sqrt),
            reads=["r_sc", "cpool"], writes=["r_sc"])
        b.op("vector", lambda e: e.tensor_scalar(
            out=r_sc[:, 2:3], in0=r_sc[:, 2:3], scalar1=-0.125,
            scalar2=None, op0=ALU.mult), reads=["r_sc"], writes=["r_sc"])
        b.op("tensor", lambda e: e.matmul(
            cx.pb[4].ap()[:, 0:1], cx.onesrow[:], r_sc[:, 2:3],
            start=True, stop=True),
            reads=["r_sc", "onesrow"], writes=["pb4"])
        b.op("scalar", lambda e, hs=hs: e.copy(
            out=t_mh[:, hs:hs + 1], in_=cx.pb[4].ap()[:, 0:1]),
            reads=["pb4"], writes=[f"t_mh{hs}"])

    av_sb1 = cx.xio
    res_sb = cx.xio.ap()[0:1, 1024:2048]

    def norm_finish1(pend):
        h, s, par = pend
        r_es = res_sb[0:1, 512 * par:512 * (par + 1)]
        b.op("tensor", lambda e, r_es=r_es: e.matmul(
            cx.pb[3][:], cx.onesrow[:], r_es, start=True, stop=True),
            reads=[f"r_es{par}", "onesrow"], writes=["pb3"])
        b.op("scalar", lambda e: e.copy(out=cx.t_ab[:], in_=cx.pb[3][:]),
             reads=["pb3"], writes=["t_ab"])
        b.op("vector", lambda e, h=h, s=s, par=par: e.tensor_tensor(
            out=ao.ap()[64 * (h % 2):64 * (h % 2) + 64,
                        1024 * (h // 2) + 512 * s:
                        1024 * (h // 2) + 512 * (s + 1)],
            in0=av_sb1[0:64, 512 * par:512 * (par + 1)],
            in1=cx.t_ab[0:64, :], op=ALU.mult),
            reads=[f"b_avs{par}", "t_ab"], writes=[f"ao{h//2}"])

    pending = []

    def hqB(h):
        hs = h % 2
        g = h // 4
        gs = g % 2
        ko = 2048 * gs
        vo = 1040 * gs
        for s in range(2):
            if pending:
                norm_finish1(pending.pop(0))
            n_kt = NKT[s]
            SCAP = [cx.pb[0].ap(), cx.pb[1].ap(), cx.pb[5].ap(),
                    cx.pbt[0].ap().bitcast(F32), cx.pbt[1].ap().bitcast(F32)]
            SCRG = ["pb0", "pb1", "pb5", "pbt0", "pbt1"]

            def sc1_stage(kt, s=s, hs=hs, gs=gs, ko=ko):
                pbi = kt % 5
                b.op("tensor", lambda e, kt=kt, pbi=pbi: e.matmul(
                    SCAP[pbi][:, 0:512],
                    k_sb[0:64, ko + 128 * kt:ko + 128 * (kt + 1)],
                    q_sb[0:64, 1024 * hs + 512 * s:
                         1024 * hs + 512 * (s + 1)],
                    start=True, stop=True),
                    reads=[f"b1k{gs}", f"b1q{hs}"], writes=[SCRG[pbi]])

            def ev1_stage(kt, s=s, gs=gs, vo=vo, hs=hs):
                pbi = kt % 5
                eo = 512 * (kt % 4)
                er = f"eb{kt % 4}"
                b.op("scalar", lambda e, eo=eo, pbi=pbi, hs=hs: e.activation(
                    out=ebuf[:, eo:eo + 512],
                    in_=SCAP[pbi][:, 0:512], func=AF.Exp,
                    bias=t_mh.ap()[:, hs:hs + 1],
                    scale=cx.cv(0.125)),
                    reads=[SCRG[pbi], f"t_mh{hs}", "cpool"],
                    writes=[er])
                masked = (kt < 8) if s == 0 else (kt >= 8)
                if masked:
                    b.op("vector", lambda e, eo=eo, kt=kt: e.tensor_tensor(
                        out=ebuf[:, eo:eo + 512],
                        in0=ebuf[:, eo:eo + 512],
                        in1=amask[:, 512 * kt:512 * (kt + 1)],
                        op=ALU.mult),
                        reads=[er, "amask"], writes=[er])
                b.op("tensor", lambda e, eo=eo, kt=kt, n=n_kt,
                     vo=vo: e.matmul(
                    cx.pb[2].ap()[0:65, :],
                    v_sb[:, vo + 65 * kt:vo + 65 * (kt + 1)],
                    ebuf[:, eo:eo + 512],
                    start=(kt == 0), stop=(kt == n - 1)),
                    reads=[f"b1v{gs}", er], writes=["pb2"])

            for kt in range(n_kt + 4):
                if kt < n_kt:
                    sc1_stage(kt)
                if kt >= 4:
                    ev1_stage(kt - 4)
            par = (2 * h + s) % 2
            r_es = res_sb[0:1, 512 * par:512 * (par + 1)]
            b.op("scalar", lambda e, par=par: e.copy(
                out=av_sb1[0:64, 512 * par:512 * (par + 1)],
                in_=cx.pb[2].ap()[0:64, :]),
                reads=["pb2"], writes=[f"b_avs{par}"])
            b.op("scalar", lambda e, r_es=r_es, par=par: e.copy(
                out=r_es, in_=cx.pb[2].ap()[64:65, :]),
                reads=["pb2"], writes=[f"r_es{par}"])
            b.op("scalar", lambda e, r_es=r_es, par=par: e.activation(
                out=r_es, in_=r_es, func=AF.Ln),
                reads=[f"r_es{par}", "cpool"], writes=[f"r_es{par}"])
            b.op("scalar", lambda e, r_es=r_es, par=par: e.activation(
                out=r_es, in_=r_es, func=AF.Exp, scale=cx.cv1(-1.0)),
                reads=[f"r_es{par}", "cpool"], writes=[f"r_es{par}"])
            pending.append((h, s, par))

    groupA(0)
    hqA(0)
    for h in range(32):
        if h + 1 < 32:
            if (h + 1) % 4 == 0:
                groupA((h + 1) // 4)
            hqA(h + 1)
        hqB(h)
    while pending:
        norm_finish1(pending.pop(0))
    stB.close()
    b.fence()
    stC = ExitStack()
    sbC = lambda n, s, d: stC.enter_context(nc.sbuf_tensor(n, s, d))
    if not fused:
        cx.xres = sbC("c1_xres", [128, NDT * 1024], F32)
    wcol2 = sbC("c1_wcol", [128, 2 * 2048], F32)
    wcol2b = sbC("c1_wcolb", [128, 2 * 2048], BF16)
    if not fused:
        emit_loadx(b, cx, io["x_own"])
    for n in range(16):
        sl = n % 2
        b.op("sync", lambda e, n=n, sl=sl: e.dma_start(
            out=wcol2[:, 2048 * sl:2048 * (sl + 1)],
            in_=io["wo1"][:, 128 * n:128 * (n + 1)].rearrange(
                "(k p) f -> p k f", p=128)),
            writes=[f"cwc{sl}"], dma_chan=f"cwc{sl}")
        b.op("vector", lambda e, sl=sl: e.tensor_copy(
            out=wcol2b[:, 2048 * sl:2048 * (sl + 1)],
            in_=wcol2[:, 2048 * sl:2048 * (sl + 1)]),
            reads=[f"cwc{sl}"], writes=[f"cwcb{sl}"])
        for c in range(2):
            pbi = c % 2
            for k in range(16):
                b.op("tensor", lambda e, sl=sl, c=c, k=k, pbi=pbi: e.matmul(
                    cx.pb[pbi][:],
                    wcol2b[:, 2048 * sl + 128 * k:2048 * sl + 128 * (k + 1)],
                    ao[:, 1024 * k + 512 * c:1024 * k + 512 * (c + 1)],
                    start=(k == 0), stop=(k == 15)),
                    reads=[f"cwcb{sl}", f"ao{k}"], writes=[f"pb{pbi}"])
            b.op("scalar", lambda e, pbi=pbi: e.copy(
                out=cx.t_tmp[:, 512 * pbi:512 * (pbi + 1)], in_=cx.pb[pbi][:]),
                reads=[f"pb{pbi}"], writes=[f"t_tmp{pbi}"])
            b.op("gpsimd", lambda e, n=n, c=c, pbi=pbi: e.tensor_tensor(
                out=cx.x(n, c), in0=cx.x(n, c),
                in1=cx.t_tmp[:, 512 * pbi:512 * (pbi + 1)], op=ALU.add),
                reads=[f"t_tmp{pbi}", f"x{n}"], writes=[f"x{n}"])
    toks = emit_storex(b, cx, io["x_out"]) if not fused else []
    stC.close()
    stAO.close()
    b.fence()
    return toks


def io_bias_col(cx, n):
    return cx.io["bqkv_sb"][:, n:n + 1]


def build_attn0_nc():
    nc = bass.Bass("TRN2", target_bir_lowering=False)
    io = {}

    def di(name, shape, dt):
        io[name] = nc.dram_tensor(name, list(shape), dt, kind="ExternalInput").ap()

    def do(name, shape, dt):
        io[name] = nc.dram_tensor(name, list(shape), dt, kind="ExternalOutput").ap()

    di("x_own", [1024, 2048], F32)
    di("x_seq", [2048, 2048], F32)
    di("wqkv", [2048, 6144], F32)
    di("bqkv_cols", [128, 48], F32)
    di("wo", [2048, 2048], F32)
    di("bo_cols", [128, 16], F32)
    di("cmask16", [16, 128, 512], BF16)
    di("cpool", [128, len(POOL_VALS)], F32)
    di("ones_bf", [128, 1], BF16)
    di("onesrow", [1, 128], F32)
    di("ident_f32", [128, 128], F32)
    di("ident_bf", [128, 128], BF16)
    do("x_out", [1024, 2048], F32)
    io["qT_d"] = nc.dram_tensor("qT_d", [16, 128, 1024], BF16).ap()
    io["kT_d"] = nc.dram_tensor("kT_d", [4, 16, 128, 512], BF16).ap()
    io["v_d"] = nc.dram_tensor("v_d", [16, 128, 2048], BF16).ap()

    def emit(b):
        cx = Ctx(b, io, 1024, alloc_stream=False, alloc_tokln=True)
        st = b._stack
        cx.id_bf = st.enter_context(nc.sbuf_tensor("s_id_bf", [128, 128], BF16))
        b.op("gpsimd", lambda e: e.dma_start(out=cx.id_bf[:], in_=io["ident_bf"]),
             writes=["id_bf"], dma_chan="c_idb")
        bq = st.enter_context(nc.sbuf_tensor("s_bqkv", [128, 48], F32))
        io["bqkv_sb"] = bq.ap()
        b.op("gpsimd", lambda e: e.dma_start(out=bq[:], in_=io["bqkv_cols"]),
             writes=["bqkv"], dma_chan="c_bq")
        bo = st.enter_context(nc.sbuf_tensor("s_bo", [128, 16], F32))
        io["bo_cols_ap"] = bo.ap()
        b.op("gpsimd", lambda e: e.dma_start(out=bo[:], in_=io["bo_cols"]),
             writes=["bo"], dma_chan="c_bo")
        toks = emit_attn0(b, cx, io)
        return [("gpsimd", toks[-1])]

    make_program(nc, emit)
    return nc


def build_ffn0_nc(Tt=T):
    nc = bass.Bass("TRN2", target_bir_lowering=False)
    io = {}

    def di(name, shape, dt):
        io[name] = nc.dram_tensor(name, list(shape), dt, kind="ExternalInput").ap()

    def do(name, shape, dt):
        io[name] = nc.dram_tensor(name, list(shape), dt, kind="ExternalOutput").ap()

    di("x_in", [Tt, 2048], F32)
    di("wgu", [128, 88, 2048], BF16)
    di("wdn", [128, 44, 2048], BF16)
    di("cpool", [128, len(POOL_VALS)], F32)
    di("ones_bf", [128, 1], BF16)
    di("onesrow", [1, 128], F32)
    di("ident_f32", [128, 128], F32)
    do("x_out", [Tt, 2048], F32)

    def emit(b):
        cx = Ctx(b, io, Tt)
        io["wgu_tile"] = lambda m: io["wgu"][:, m, :]
        io["wdn_tile"] = lambda oc, j2: io["wdn"][
            :, 11 * oc:11 * (oc + 1), 256 * j2:256 * (j2 + 1)]
        emit_loadx(b, cx, io["x_in"])
        emit_ln(b, cx)
        emit_ffn0(b, cx, io)
        toks = emit_storex(b, cx, io["x_out"])
        return [("gpsimd", toks[-1])]

    make_program(nc, emit)
    return nc


def build_ffn1_nc(Tt=T):
    nc = bass.Bass("TRN2", target_bir_lowering=False)
    io = {}

    def di(name, shape, dt):
        io[name] = nc.dram_tensor(name, list(shape), dt, kind="ExternalInput").ap()

    def do(name, shape, dt):
        io[name] = nc.dram_tensor(name, list(shape), dt, kind="ExternalOutput").ap()

    di("x_in", [Tt, 2048], F32)
    di("wug", [128, 128, 2048], BF16)
    di("wdn1", [128, 64, 2048], BF16)
    di("wsu_cols", [128, 64], F32)
    di("wsg_cols", [128, 64], F32)
    di("wsd_cols", [128, 16], F32)
    di("cpool", [128, len(POOL_VALS)], F32)
    di("ones_bf", [128, 1], BF16)
    di("onesrow", [1, 128], F32)
    di("ident_f32", [128, 128], F32)
    do("x_out", [Tt, 2048], F32)

    def emit(b):
        cx = Ctx(b, io, Tt)
        st = b._stack
        for nm, reg, scl in ([] if "ws" in _skip else [
                             ("wsu_cols", "wsu", SCALE1),
                             ("wsg_cols", "wsg", SCALE1),
                             ("wsd_cols", "wsd", 0.5 * SCALE1)]):
            t = st.enter_context(nc.sbuf_tensor(f"s_{reg}", [128,
                io[nm].shape[1]], F32))
            io[f"{reg}_sb"] = t.ap()
            b.op("gpsimd", lambda e, t=t, nm=nm: e.dma_start(
                out=t[:], in_=io[nm]), writes=[reg], dma_chan=f"c_{reg}")
            b.op("vector", lambda e, t=t, scl=scl: e.tensor_scalar(
                out=t[:], in0=t[:], scalar1=scl, scalar2=None, op0=ALU.mult),
                reads=[reg], writes=[reg])
        io["wug_tile"] = lambda m: io["wug"][:, m, :]
        io["wdn1_tile"] = lambda oc, j2: io["wdn1"][
            :, 16 * oc:16 * (oc + 1), 256 * j2:256 * (j2 + 1)]
        emit_loadx(b, cx, io["x_in"])
        emit_ln(b, cx)
        emit_ffn1(b, cx, io)
        toks = emit_storex(b, cx, io["x_out"])
        return [("gpsimd", toks[-1])]

    make_program(nc, emit)
    return nc


def build_attn1_nc():
    nc = bass.Bass("TRN2", target_bir_lowering=False)
    io = {}

    def di(name, shape, dt):
        io[name] = nc.dram_tensor(name, list(shape), dt, kind="ExternalInput").ap()

    def do(name, shape, dt):
        io[name] = nc.dram_tensor(name, list(shape), dt, kind="ExternalOutput").ap()

    di("x_own", [1024, 2048], F32)
    di("x_seq", [2048, 2048], F32)
    di("wq", [2048, 2048], F32)
    di("wkv", [2048, 1024], F32)
    di("wo1", [2048, 2048], F32)
    di("cmask16", [16, 128, 512], BF16)
    di("cos_own", [128, 1024], F32)
    di("sin_own", [128, 1024], F32)
    di("cos_seq", [128, 2048], F32)
    di("sin_seq", [128, 2048], F32)
    di("ropeRT", [128, 128], BF16)
    di("cpool", [128, len(POOL_VALS)], F32)
    di("ones_bf", [128, 1], BF16)
    di("onesrow", [1, 128], F32)
    di("ident_f32", [128, 128], F32)
    di("ident_bf", [128, 128], BF16)
    do("x_out", [1024, 2048], F32)
    io["qT_d"] = nc.dram_tensor("qT_d", [16, 128, 1024], BF16).ap()
    io["kT_d"] = nc.dram_tensor("kT_d", [4, 4, 128, 512], BF16).ap()
    io["v_d1"] = nc.dram_tensor("v_d1", [16, 128, 512], BF16).ap()

    def emit(b):
        cx = Ctx(b, io, 1024, alloc_stream=False, alloc_tokln=True)
        st = b._stack
        cx.id_bf = st.enter_context(nc.sbuf_tensor("s_id_bf", [128, 128], BF16))
        b.op("gpsimd", lambda e: e.dma_start(out=cx.id_bf[:], in_=io["ident_bf"]),
             writes=["id_bf"], dma_chan="c_idb")
        cx.ropeRT = st.enter_context(nc.sbuf_tensor("s_ropeRT", [128, 128], BF16))
        b.op("gpsimd", lambda e: e.dma_start(out=cx.ropeRT[:], in_=io["ropeRT"]),
             writes=["ropeRT"], dma_chan="c_rrt")
        toks = emit_attn1(b, cx, io)
        return [("gpsimd", toks[-1])]

    make_program(nc, emit)
    return nc


def build_fused_nc():
    nc = bass.Bass("TRN2", target_bir_lowering=False)
    io = {}

    def di(name, shape, dt):
        io[name] = nc.dram_tensor(name, list(shape), dt, kind="ExternalInput").ap()

    def dint(name, shape, dt, shared=False):
        kw = {"addr_space": "Shared"} if shared else {}
        io[name] = nc.dram_tensor(name, list(shape), dt, **kw).ap()

    di("x_own", [1024, 2048], F32)
    di("h128", [128, 128], BF16)
    di("h16i8", [128, 128], BF16)
    di("h64i2", [128, 128], BF16)
    di("su_ug_h", [128, 16], F32)
    di("su_dp_h", [128, 64], F32)
    di("gp", [1408, 256], I32)
    di("gs", [1408, 16], F16)
    di("dp", [256, 704], I32)
    di("ds", [256, 44], F16)
    di("wup", [1024, 2048], F32)
    di("wgate", [1024, 2048], F32)
    di("wdn1w", [256, 8192], F32)
    di("wqkv", [2048, 6144], F32)
    di("bqkv_cols", [128, 48], F32)
    di("wo", [2048, 2048], F32)
    di("bo_cols", [128, 16], F32)
    di("wq", [2048, 2048], F32)
    di("wkv", [2048, 1024], F32)
    di("wo1", [2048, 2048], F32)
    di("wsu_cols", [128, 64], F32)
    di("wsg_cols", [128, 64], F32)
    di("wsd_cols", [128, 16], F32)
    di("cmask16", [16, 128, 512], BF16)
    di("cos_own", [128, 1024], F32)
    di("sin_own", [128, 1024], F32)
    di("ropeRT", [128, 128], BF16)
    di("cpool", [128, len(POOL_VALS)], F32)
    di("ones_bf", [128, 1], BF16)
    di("onesrow", [1, 128], F32)
    di("ident_f32", [128, 128], F32)
    di("ident_bf", [128, 128], BF16)
    x_out = nc.dram_tensor("x_out", [1024, 2048], F32, kind="ExternalOutput").ap()

    dint("wgu_out", [128, 11, 2048], BF16)
    dint("wdn_out", [128, 44, 256], BF16)
    dint("wug_out", [128, 16, 2048], BF16)
    dint("wdn1_out", [128, 64, 256], BF16)
    dint("y1a", [16, 128, 1024], BF16)
    dint("y1b", [16, 128, 1024], BF16)
    dint("y1d", [64, 128, 256], BF16)
    dint("wgu_ag", [8 * 128, 16 * 1408], BF16)
    dint("wdn_ag", [8 * 128, 44 * 256], BF16)
    dint("wug_ag", [8 * 128, 16 * 2048], BF16)
    dint("wdn1_ag", [8 * 128, 64 * 256], BF16)
    dint("qT_d", [16, 128, 1024], BF16)
    dint("kT_own", [2, 16, 128, 512], BF16)
    dint("v_own", [2, 4, 128, 2048], BF16)
    dint("kT_ag", [2 * 2 * 16 * 128, 512], BF16)
    dint("v_ag", [2 * 2 * 4 * 128, 2048], BF16)
    dint("kT1_own", [2, 4, 128, 512], BF16)
    dint("v1_own", [2, 4, 128, 512], BF16)
    dint("kT1_ag", [2 * 2 * 4 * 128, 512], BF16)
    dint("v1_ag", [2 * 2 * 4 * 128, 512], BF16)

    io["pair_groups"] = [[0, 1], [2, 3], [4, 5], [6, 7]]
    io["gmap"] = {(0, 0): 0, (1, 0): 1, (1, 1): 2, (0, 1): 3}
    io["kT_own_flat"] = io["kT_own"].rearrange("s h p t -> (s h p) t")
    io["v_own_flat"] = io["v_own"].rearrange("s q p t -> (s q p) t")
    io["kT1_own_flat"] = io["kT1_own"].rearrange("s h p t -> (s h p) t")
    io["v1_own_flat"] = io["v1_own"].rearrange("s q p t -> (s q p) t")
    io["kT_ag_flat"] = io["kT_ag"]
    io["v_ag_flat"] = io["v_ag"]
    io["kT1_ag_flat"] = io["kT1_ag"]
    io["v1_ag_flat"] = io["v1_ag"]
    io["kT_ag"] = io["kT_ag"].rearrange(
        "(r s h p) t -> r s h p t", r=2, s=2, h=16)
    io["v_ag"] = io["v_ag"].rearrange(
        "(r s q p) t -> r s q p t", r=2, s=2, q=4)
    io["kT1_ag"] = io["kT1_ag"].rearrange(
        "(r s h p) t -> r s h p t", r=2, s=2, h=4)
    io["v1_ag"] = io["v1_ag"].rearrange(
        "(r s q p) t -> r s q p t", r=2, s=2, q=4)

    wgu_ag_v = io["wgu_ag"].rearrange("(r p) (m k) -> r p m k", p=128, k=2048)
    wdn_ag_v = io["wdn_ag"].rearrange("(r p) (k m) -> r p k m", p=128, m=256)
    wug_ag_v = io["wug_ag"].rearrange("(r p) (m k) -> r p m k", p=128, k=2048)
    wdn1_ag_v = io["wdn1_ag"].rearrange("(r p) (k m) -> r p k m", p=128, m=256)

    def wgu_tile(m):
        r, ml = m // 11, m % 11
        return wgu_ag_v[r][:, ml, :]

    def wdn_tile(oc, j2):
        return wdn_ag_v[j2][:, 11 * oc:11 * (oc + 1), :]

    def wug_tile(m):
        if m < 64:
            r, ml = m // 8, m % 8
        else:
            r, ml = (m - 64) // 8, 8 + (m - 64) % 8
        return wug_ag_v[r][:, ml, :]

    def wdn1_tile(oc, j2):
        return wdn1_ag_v[j2][:, 16 * oc:16 * (oc + 1), :]

    io["wgu_tile"] = wgu_tile
    io["wdn_tile"] = wdn_tile
    io["wug_tile"] = wug_tile
    io["wdn1_tile"] = wdn1_tile
    io["wdn1"] = io["wdn1w"]

    def emit(b):
        cx = Ctx(b, io, T)
        b._cx_pbt = cx.pbt
        b._cx_pb = cx.pb
        st = b._stack
        import os as _os2
        _skip = set(_os2.environ.get("FUSED_SKIP", "").split(","))
        cx.id_bf = st.enter_context(nc.sbuf_tensor("s_id_bf", [128, 128], BF16))
        b.op("gpsimd", lambda e: e.dma_start(out=cx.id_bf[:], in_=io["ident_bf"]),
             writes=["id_bf"], dma_chan="c_idb")
        cx.ropeRT = st.enter_context(nc.sbuf_tensor("s_ropeRT", [128, 128], BF16))
        b.op("gpsimd", lambda e: e.dma_start(out=cx.ropeRT[:], in_=io["ropeRT"]),
             writes=["ropeRT"], dma_chan="c_rrt")
        bq = st.enter_context(nc.sbuf_tensor("s_bqkv", [128, 48], F32))
        io["bqkv_sb"] = bq.ap()
        b.op("gpsimd", lambda e: e.dma_start(out=bq[:], in_=io["bqkv_cols"]),
             writes=["bqkv"], dma_chan="c_bq")
        bo = st.enter_context(nc.sbuf_tensor("s_bo", [128, 16], F32))
        io["bo_cols_ap"] = bo.ap()
        b.op("gpsimd", lambda e: e.dma_start(out=bo[:], in_=io["bo_cols"]),
             writes=["bo"], dma_chan="c_bo")
        for nm, reg, scl in ([] if "ws" in _skip else [
                             ("wsu_cols", "wsu", SCALE1),
                             ("wsg_cols", "wsg", SCALE1),
                             ("wsd_cols", "wsd", 0.5 * SCALE1)]):
            t = st.enter_context(nc.sbuf_tensor(f"s_{reg}", [128,
                io[nm].shape[1]], F32))
            io[f"{reg}_sb"] = t.ap()
            b.op("gpsimd", lambda e, t=t, nm=nm: e.dma_start(
                out=t[:], in_=io[nm]), writes=[reg], dma_chan=f"c_{reg}")
            b.op("vector", lambda e, t=t, scl=scl: e.tensor_scalar(
                out=t[:], in0=t[:], scalar1=scl, scalar2=None, op0=ALU.mult),
                reads=[reg], writes=[reg])
        import os
        b.op("vector", lambda e: e.memset(cx.t_ab[:, 0:1], 0.0),
             writes=["t_ab"])
        b.op("scalar", lambda e: e.memzero(cx.t_ab[:, 0:1]), reads=["t_ab"],
             writes=["t_ab"])
        b.op("tensor", lambda e: e.transpose(
            cx.pbt[0].ap()[:, 0:128], cx.id_bf[:], cx.id_bf[:]),
            reads=["id_bf"], writes=["pbt0"])
        if os.environ.get("FUSED_NOX", "0") != "1":
            emit_loadx(b, cx, io["x_own"])
        grp8 = [list(range(8))]

        def emit_ag(nm_sl, nm_ag, ch):
            in_ap = io[nm_sl].rearrange("p k m -> p (k m)")
            out_ap = {"wgu_ag": io["wgu_ag"], "wdn_ag": io["wdn_ag"],
                      "wug_ag": io["wug_ag"], "wdn1_ag": io["wdn1_ag"]}[nm_ag]
            b.op("gpsimd", lambda e, a=in_ap, o=out_ap: e.collective_compute(
                "AllGather", ALU.bypass, replica_groups=grp8,
                ins=[a.opt()], outs=[o.opt()]),
                writes=[f"ag_{nm_ag}"], cc_chan=ch)

        def mid():
            emit_ag("wgu_out", "wgu_ag", "cc_wgu")
            emit_ag("wdn_out", "wdn_ag", "cc_wdn")

        if "prep" not in _skip:
            io_prep = dict(io)
            io_prep["wdn1"] = io["wdn1w"]
            import os as _os
            _pp = tuple(p for p in _os.environ.get("FUSED_PREP", "dq,fa,fb").split(",") if p)
            emit_prep(b, io_prep, _pp, mid_cb=mid)
        b.fence()
        emit_ag("wug_out", "wug_ag", "cc_wug")
        emit_ag("wdn1_out", "wdn1_ag", "cc_wd1")
        nsub = int(os.environ.get("FUSED_NSUB", "6"))
        io["w_regions"] = ["ag_wgu_ag", "ag_wdn_ag"]
        if nsub >= 1:
            emit_ln(b, cx)
            emit_ffn0(b, cx, io, tag="f0a")
        if nsub >= 2:
            emit_ln(b, cx)
            emit_attn0(b, cx, io, fused=True)
        if nsub >= 3:
            emit_ln(b, cx)
            emit_ffn0(b, cx, io, tag="f0b")
        io["w_regions"] = ["ag_wug_ag", "ag_wdn1_ag"]
        if nsub >= 4:
            emit_ln(b, cx)
            emit_ffn1(b, cx, io, tag="f1a")
        if nsub >= 5:
            emit_ln(b, cx)
            emit_attn1(b, cx, io, fused=True)
        if nsub >= 6:
            emit_ln(b, cx)
            emit_ffn1(b, cx, io, tag="f1b")
        if os.environ.get("FUSED_NOX", "0") != "1":
            toks = emit_storex(b, cx, x_out)
        else:
            toks = [b.op("gpsimd", lambda e: e.dma_start(
                out=x_out[0:128, :], in_=cx.xres[:, 0:2048]),
                reads=["x0"], dma_chan="xout0")]
        fw = [("gpsimd", toks[-1])]
        for ch in list(b.chans):
            if ch.startswith("cc_"):
                fw.append(("gpsimd", ("D", ch, b.chans[ch], None, None)))
        return fw

    make_program(nc, emit)
    return nc


def kernel_fused(inputs):
    consts = host_prep_consts()
    cores = list(range(N_CORES))
    onesrow = np.ones((1, 128), np.float32)
    nc = _get_nc("fused", build_fused_nc)
    x = np.ascontiguousarray(inputs["L0_attn_x"]).astype(np.float32)
    ws = inputs["L1_ffn_Wscale_ug"]
    in_maps = []
    for r in cores:
        w = host_prep_weights(inputs, r)
        rows = _own_rows(r)
        in_maps.append({
            "x_own": np.ascontiguousarray(x[r // 2][rows]),
            "h128": consts["h128"], "h16i8": consts["h16i8"],
            "h64i2": consts["h64i2"],
            "su_ug_h": w["su_ug_h"], "su_dp_h": w["su_dp_h"],
            "gp": w["gp"], "gs": w["gs"], "dp": w["dp"], "ds": w["ds"],
            "wup": w["wup"], "wgate": w["wgate"], "wdn1w": w["wdn1"],
            "wqkv": inputs["L0_attn_Wqkv"],
            "bqkv_cols": _col_tiles(inputs["L0_attn_bqkv"]),
            "wo": inputs["L0_attn_Wo"],
            "bo_cols": _col_tiles(inputs["L0_attn_bo"]),
            "wq": inputs["L1_attn_Wq"], "wkv": inputs["L1_attn_Wkv"],
            "wo1": inputs["L1_attn_Wo"],
            "wsu_cols": _col_tiles(ws[:8192]),
            "wsg_cols": _col_tiles(ws[8192:]),
            "wsd_cols": _col_tiles(inputs["L1_ffn_Wscale_dp"]),
            "cmask16": _core_masks(r),
            "cos_own": np.ascontiguousarray(consts["cos_rep"][:, rows]),
            "sin_own": np.ascontiguousarray(consts["sin_rep"][:, rows]),
            "ropeRT": consts["ropeRT"],
            "cpool": consts["cpool"], "ones_bf": consts["ones_bf"],
            "onesrow": onesrow, "ident_f32": consts["ident_f32"],
            "ident_bf": consts["ident_bf"],
        })
    res = run_bass_kernel_spmd(nc, in_maps, core_ids=cores)
    for r in cores:
        x[r // 2][_own_rows(r)] = res.results[r]["x_out"]
    return x


_NC_CACHE = {}


def _get_nc(name, builder):
    if name not in _NC_CACHE:
        _NC_CACHE[name] = builder()
    return _NC_CACHE[name]


def _core_chunks(r):
    return (0, 3) if r % 2 == 0 else (1, 2)


def _own_rows(r):
    c0, c1 = _core_chunks(r)
    return np.r_[512 * c0:512 * (c0 + 1), 512 * c1:512 * (c1 + 1)]


def _core_masks(r):
    chunks = _core_chunks(r)
    kl = np.arange(128)[:, None]
    ql = np.arange(512)[None, :]
    cm = np.zeros((16, 128, 512), np.float32)
    for j in range(16):
        qg = chunks[0 if j < 8 else 1]
        cm[j] = (128 * j + kl <= 512 * qg + ql)
    return cm.astype(bfnp)


def kernel(**inputs):
    import os
    if os.environ.get("MK_UNFUSED", "0") != "1":
        return kernel_fused(inputs)
    consts = host_prep_consts()
    cores = list(range(N_CORES))
    onesrow = np.ones((1, 128), np.float32)

    prep_nc = _get_nc("prep", build_prep_nc)
    in_maps = []
    for r in cores:
        w = host_prep_weights(inputs, r)
        in_maps.append({
            "h128": consts["h128"], "h16i8": consts["h16i8"],
            "h64i2": consts["h64i2"], "ident_bf": consts["ident_bf"],
            "su_ug_h": w["su_ug_h"], "su_dp_h": w["su_dp_h"],
            "gp": w["gp"], "gs": w["gs"], "dp": w["dp"], "ds": w["ds"],
            "wup": w["wup"], "wgate": w["wgate"], "wdn1": w["wdn1"],
        })
    res = run_bass_kernel_spmd(prep_nc, in_maps, core_ids=cores)
    rr = res.results
    wgu = np.concatenate([rr[r]["wgu_out"] for r in cores], axis=1)
    wdn = np.concatenate([rr[r]["wdn_out"] for r in cores], axis=2)
    wdn1 = np.concatenate([rr[r]["wdn1_out"] for r in cores], axis=2)
    wug = np.zeros((128, 128, 2048), dtype=bfnp)
    for r in cores:
        blk = rr[r]["wug_out"]
        wug[:, 8 * r:8 * (r + 1), :] = blk[:, 0:8, :]
        wug[:, 64 + 8 * r:64 + 8 * (r + 1), :] = blk[:, 8:16, :]

    x = np.ascontiguousarray(inputs["L0_attn_x"]).astype(np.float32)

    common = {"cpool": consts["cpool"], "ones_bf": consts["ones_bf"],
              "onesrow": onesrow, "ident_f32": consts["ident_f32"]}

    def run_ffn0():
        nonlocal x
        nc = _get_nc("ffn0", lambda: build_ffn0_nc(T))
        ims = [{"x_in": np.ascontiguousarray(x[r // 2][_own_rows(r)]),
                "wgu": wgu, "wdn": wdn, **common} for r in cores]
        out = run_bass_kernel_spmd(nc, ims, core_ids=cores).results
        for r in cores:
            x[r // 2][_own_rows(r)] = out[r]["x_out"]

    def run_attn0():
        nonlocal x
        nc = _get_nc("attn0", build_attn0_nc)
        ims = [{"x_own": np.ascontiguousarray(x[r // 2][_own_rows(r)]),
                "x_seq": x[r // 2],
                "wqkv": inputs["L0_attn_Wqkv"],
                "bqkv_cols": _col_tiles(inputs["L0_attn_bqkv"]),
                "wo": inputs["L0_attn_Wo"],
                "bo_cols": _col_tiles(inputs["L0_attn_bo"]),
                "cmask16": _core_masks(r),
                "ident_bf": consts["ident_bf"], **common} for r in cores]
        out = run_bass_kernel_spmd(nc, ims, core_ids=cores).results
        for r in cores:
            x[r // 2][_own_rows(r)] = out[r]["x_out"]

    def run_ffn1():
        nonlocal x
        nc = _get_nc("ffn1", lambda: build_ffn1_nc(T))
        ws = inputs["L1_ffn_Wscale_ug"]
        ims = [{"x_in": np.ascontiguousarray(x[r // 2][_own_rows(r)]),
                "wug": wug, "wdn1": wdn1,
                "wsu_cols": _col_tiles(ws[:8192]),
                "wsg_cols": _col_tiles(ws[8192:]),
                "wsd_cols": _col_tiles(inputs["L1_ffn_Wscale_dp"]),
                **common} for r in cores]
        out = run_bass_kernel_spmd(nc, ims, core_ids=cores).results
        for r in cores:
            x[r // 2][_own_rows(r)] = out[r]["x_out"]

    def run_attn1():
        nonlocal x
        nc = _get_nc("attn1", build_attn1_nc)
        ims = []
        for r in cores:
            rows = _own_rows(r)
            ims.append({
                "x_own": np.ascontiguousarray(x[r // 2][rows]),
                "x_seq": x[r // 2],
                "wq": inputs["L1_attn_Wq"], "wkv": inputs["L1_attn_Wkv"],
                "wo1": inputs["L1_attn_Wo"],
                "cmask16": _core_masks(r),
                "cos_own": np.ascontiguousarray(consts["cos_rep"][:, rows]),
                "sin_own": np.ascontiguousarray(consts["sin_rep"][:, rows]),
                "cos_seq": consts["cos_rep"], "sin_seq": consts["sin_rep"],
                "ropeRT": consts["ropeRT"],
                "ident_bf": consts["ident_bf"], **common})
        out = run_bass_kernel_spmd(nc, ims, core_ids=cores).results
        for r in cores:
            x[r // 2][_own_rows(r)] = out[r]["x_out"]

    run_ffn0()
    run_attn0()
    run_ffn0()
    run_ffn1()
    run_attn1()
    run_ffn1()
    return x


if __name__ == "__main__":
    from concourse.bass_interp import MultiCoreSim
    rng = np.random.default_rng(0)
    X = rng.standard_normal((128, 256)).astype(np.float32)
    W = rng.standard_normal((128, 64)).astype(np.float32)
    exp = np.maximum(
        W.astype(bfnp).astype(np.float32).T @ X.astype(bfnp).astype(np.float32), 0.0)

    nc = bass.Bass("TRN2", target_bir_lowering=False)
    x_in = nc.dram_tensor("x", [128, 256], F32, kind="ExternalInput").ap()
    w_in = nc.dram_tensor("w", [128, 64], F32, kind="ExternalInput").ap()
    y_out = nc.dram_tensor("y", [64, 256], F32, kind="ExternalOutput").ap()

    def emit(b):
        nc_ = b.nc
        st = b._stack
        xt = st.enter_context(nc_.sbuf_tensor("xt", [128, 256], F32))
        wt = st.enter_context(nc_.sbuf_tensor("wt", [128, 64], F32))
        xb = st.enter_context(nc_.sbuf_tensor("xb", [128, 256], BF16))
        wb = st.enter_context(nc_.sbuf_tensor("wb", [128, 64], BF16))
        yp = st.enter_context(nc_.psum_tensor("yp", [64, 512], F32))
        yt = st.enter_context(nc_.sbuf_tensor("yt", [64, 256], F32))
        b.op("sync", lambda e: e.dma_start(out=xt[:], in_=x_in), writes=["xt"], dma_chan="ld0")
        b.op("sync", lambda e: e.dma_start(out=wt[:], in_=w_in), writes=["wt"], dma_chan="ld1")
        b.op("vector", lambda e: e.tensor_copy(out=xb[:], in_=xt[:]),
             reads=["xt"], writes=["xb"])
        b.op("vector", lambda e: e.tensor_copy(out=wb[:], in_=wt[:]),
             reads=["wt"], writes=["wb"])
        b.op("tensor", lambda e: e.matmul(yp[:, 0:256], wb[:], xb[:],
                                          start=True, stop=True),
             reads=["xb", "wb"], writes=["yp"])
        b.op("scalar", lambda e: e.activation(out=yt[:], in_=yp[:, 0:256],
                                              func=AF.Relu),
             reads=["yp"], writes=["yt"])
        tok = b.op("sync", lambda e: e.dma_start(out=y_out, in_=yt[:]),
                   reads=["yt"], dma_chan="st")
        return [("sync", tok)]

    make_program(nc, emit)
    sim = MultiCoreSim(nc, num_cores=1, num_workers=1, trace=False)
    sim.cores[0].tensor("x")[:] = X
    sim.cores[0].tensor("w")[:] = W
    sim.simulate(check_with_hw=False)
    got = np.asarray(sim.cores[0].tensor("y"))
    err = np.abs(got - exp).max()
    print("framework smoke maxerr:", err, "OK" if err < 1e-5 else "FAIL")
